# version 06
# baseline (speedup 1.0000x reference)
import numpy as np
import ml_dtypes
import concourse.bass as bass
import concourse.mybir as mybir
from concourse.bass_utils import run_bass_kernel_spmd
from contextlib import ExitStack

F32 = mybir.dt.float32
BF16 = mybir.dt.bfloat16
AF = mybir.ActivationFunctionType
ALU = mybir.AluOpType
AX = mybir.AxisListType

D = 2048
DC = 16
T = 1024
NT = 8
H = 16
HD = 128
FFC = 64
FSC = 22
FEC = 11
E = 8
QSCALE = 1.0 / np.sqrt(128.0)
C1 = 0.5 / np.sqrt(8.0)

BF = ml_dtypes.bfloat16
DEBUG_ATTN = False
DBG_L = 0
AG_OVERLAP = False



def core_tiles(par):
    return list(range(1, 16, 2)) if par == 0 else list(range(0, 16, 2))


def host_prep(inp):
    x = np.asarray(inp["L0_attn_x"])
    slopes = 2.0 ** (-(np.arange(1, 17)) / 2.0)
    tri = np.triu(np.ones((128, 128), np.float32))
    onesm = np.ones((128, 128), np.float32)
    zerosm = np.zeros((128, 128), np.float32)

    def br(b, n):
        return np.ascontiguousarray(np.asarray(b).reshape(n, 128).T)

    shared = {
        "wqkv": np.asarray(inp["L0_attn_w_qkv"]),
        "wproj": np.asarray(inp["L0_attn_w_proj"]),
        "wff1": np.asarray(inp["L0_attn_w_ff1"]),
        "bff1": br(inp["L0_attn_b_ff1"], 64),
        "wff2": np.asarray(inp["L0_attn_w_ff2"]),
        "bff2": br(inp["L0_attn_b_ff2"], 16),
        "bffn0": br(inp["L0_ffn_bias"], 16),
        "wq": np.asarray(inp["L1_attn_Wq"]),
        "wk": np.asarray(inp["L1_attn_Wk"]),
        "wv": np.asarray(inp["L1_attn_Wv"]),
        "wo": np.asarray(inp["L1_attn_Wo"]),
        "bq": br(inp["L1_attn_bq"], 16),
        "bk": br(inp["L1_attn_bk"], 16),
        "bo": br(inp["L1_attn_bo"], 16),
        "bvrow": np.asarray(inp["L1_attn_bv"]).reshape(1, 2048),
        "wrt": np.ascontiguousarray(np.asarray(inp["L1_ffn_Wr"]).T),
        "wsgt": np.ascontiguousarray(np.asarray(inp["L1_ffn_Wsg"]).T),
        "wsut": np.ascontiguousarray(np.asarray(inp["L1_ffn_Wsu"]).T),
        "wsdt": np.ascontiguousarray(np.asarray(inp["L1_ffn_Wsd"]).T),
        "wgt": np.ascontiguousarray(np.asarray(inp["L1_ffn_Wg"]).transpose(0, 2, 1)),
        "wut": np.ascontiguousarray(np.asarray(inp["L1_ffn_Wu"]).transpose(0, 2, 1)),
        "wdt": np.ascontiguousarray(np.asarray(inp["L1_ffn_Wd"]).transpose(0, 2, 1)),
    }
    acol = np.zeros((128, 256), np.float32)
    for h in range(16):
        for kc in range(16):
            acol[:, h * 16 + kc] = slopes[h] * ((kc * 128 + np.arange(128)) - 2047)
    shared["acol"] = acol

    factors = np.asarray(inp["L0_ffn_factors"])
    maps = []
    for c in range(8):
        s, par = c // 2, c % 2
        tiles = core_tiles(par)
        toks = np.concatenate([np.arange(t * 128, (t + 1) * 128) for t in tiles])
        m = dict(shared)
        m["xs"] = np.ascontiguousarray(x[s][toks].T)
        m["fac"] = np.ascontiguousarray(factors[:, 256 * c:256 * (c + 1), :])
        if par == 0:
            m["maskA"], m["maskB"] = onesm, tri
            m["amaskA"], m["amaskB"] = zerosm, (1.0 - tri) * -1e30
        else:
            m["maskA"], m["maskB"] = tri, zerosm
            m["amaskA"], m["amaskB"] = (1.0 - tri) * -1e30, np.full((128, 128), -1e30, np.float32)
        crow = np.zeros((16, 1024), np.float32)
        for h in range(16):
            for j, t in enumerate(tiles):
                crow[h, j * 128:(j + 1) * 128] = slopes[h] * (t * 128 + np.arange(128) - 2047)
        m["crow"] = crow.reshape(1, 16 * 1024)
        maps.append(m)
    return maps


def host_assemble(results):
    out = np.empty((4, 2048, 2048), np.float32)
    for c in range(8):
        s, par = c // 2, c % 2
        tiles = core_tiles(par)
        toks = np.concatenate([np.arange(t * 128, (t + 1) * 128) for t in tiles])
        out[s][toks] = results[c]["r_out"].T
    return out



class Sem:
    __slots__ = ("h", "n")

    def __init__(self, h):
        self.h = h
        self.n = 0

    def tok(self, instr, k=1):
        instr.then_inc(self.h, k)
        self.n += k
        return (self, self.n)


def waits(eng, toks):
    best = {}
    for t in toks:
        if t is None:
            continue
        s, v = t
        if v > best.get(id(s), (None, 0))[1]:
            best[id(s)] = (s, v)
    for s, v in best.values():
        eng.wait_ge(s.h, v)


class Trk:
    __slots__ = ("w", "r")

    def __init__(self):
        self.w = None
        self.r = []

    def rd(self, eng):
        waits(eng, [self.w])

    def rdone(self, tok):
        self.r.append(tok)

    def wr(self, eng):
        waits(eng, self.r + [self.w])

    def wdone(self, tok):
        self.w = tok
        self.r = []


class Ctx:
    pass


PH_INPUTS = {
    "pre0": ["fac", "bffn0"],
    "attn0": ["wqkv", "wproj", "wff1", "bff1", "wff2", "bff2", "maskA", "maskB"],
    "post0": ["fac", "bffn0"],
    "moe": ["wrt", "wsgt", "wsut", "wsdt", "wgt", "wut", "wdt"],
    "attn1": ["wq", "wk", "wv", "wo", "bq", "bk", "bo", "bvrow", "amaskA", "amaskB",
              "acol", "crow"],
}

IN_SHAPES = {
    "xs": ([2048, 1024], F32), "fac": ([8, 256, 2048], F32),
    "wqkv": ([2048, 6144], F32), "wproj": ([2048, 2048], F32),
    "wff1": ([2048, 8192], F32), "bff1": ([128, 64], F32),
    "wff2": ([8192, 2048], F32), "bff2": ([128, 16], F32),
    "bffn0": ([128, 16], F32),
    "wq": ([2048, 2048], F32), "wk": ([2048, 2048], F32), "wv": ([2048, 2048], F32),
    "wo": ([2048, 2048], F32), "bq": ([128, 16], F32), "bk": ([128, 16], F32),
    "bo": ([128, 16], F32), "bvrow": ([1, 2048], F32),
    "wrt": ([2048, 8], F32), "wsgt": ([2048, 2816], F32), "wsut": ([2048, 2816], F32),
    "wsdt": ([2816, 2048], F32), "wgt": ([8, 2048, 1408], F32),
    "wut": ([8, 2048, 1408], F32), "wdt": ([8, 1408, 2048], F32),
    "maskA": ([128, 128], F32), "maskB": ([128, 128], F32),
    "acol": ([128, 256], F32), "crow": ([1, 16384], F32),
    "amaskA": ([128, 128], F32), "amaskB": ([128, 128], F32),
}


def build_program(phases, first=False, last=False):
    nc = bass.Bass("TRN2", target_bir_lowering=False)
    c = Ctx()
    c.nc = nc
    c.phases = phases

    need = set()
    for p in phases:
        need.update(PH_INPUTS[p])
    if first:
        need.add("xs")
    c.inputs_needed = sorted(need) + ([] if first else ["r_in"])

    dram_in = {}
    for nm in sorted(need):
        sh, dt = IN_SHAPES[nm]
        dram_in[nm] = nc.dram_tensor(nm, sh, dt, kind="ExternalInput").ap()
    if not first:
        dram_in["r_in"] = nc.dram_tensor("r_in", [2048, 1024], F32, kind="ExternalInput").ap()
    c.din = dram_in
    c.r_out = nc.dram_tensor("r_out", [2048, 1024], F32, kind="ExternalOutput").ap()
    if DEBUG_ATTN:
        c.dbg_q = nc.dram_tensor("dbg_q", [2048, 1024], F32, kind="ExternalOutput").ap()
        c.dbg_ctx = nc.dram_tensor("dbg_ctx", [2048, 1024], F32, kind="ExternalOutput").ap()
        c.dbg_kvg = nc.dram_tensor("dbg_kvg", [8192, 1024], F32, kind="ExternalOutput").ap()
        c.dbg_kvc = nc.dram_tensor("dbg_kvc", [4096, 1024], F32, kind="ExternalOutput").ap()
        c.dbg_h0 = nc.dram_tensor("dbg_h0", [1024, 1024], F32, kind="ExternalOutput").ap()
        c.dbg_w64 = nc.dram_tensor("dbg_w64", [128, 64], F32, kind="ExternalOutput").ap()
        c.dbg_rsh = nc.dram_tensor("dbg_rsh", [2048, 1024], F32, kind="ExternalOutput").ap()
        c.dbg_rff = nc.dram_tensor("dbg_rff", [2048, 1024], F32, kind="ExternalOutput").ap()

    if "pre0" in phases or "post0" in phases:
        c.fslice = nc.dram_tensor("fslice", [256, 2048], BF16)
        c.fgath = nc.dram_tensor("fgath", [2048, 2048], BF16)
    for L, ph in ((0, "attn0"), (1, "attn1")):
        if ph in phases:
            setattr(c, f"kvcon{L}", nc.dram_tensor(f"kvcon{L}", [4096, 1024], BF16))
            setattr(c, f"kvg{L}", nc.dram_tensor(f"kvg{L}", [8192, 1024], BF16))

    stk = ExitStack()
    c.stk = stk
    e = stk.enter_context
    c.R = e(nc.sbuf_tensor("R", [128, 16, 1024], F32))
    c.XN = e(nc.sbuf_tensor("XN", [128, 16, 1024], BF16))
    c.BIGA = e(nc.sbuf_tensor("BIGA", [128, 16, 1024], BF16))
    c.WS = e(nc.sbuf_tensor("WS", [128, 4, 2048], BF16))
    c.SQ = e(nc.sbuf_tensor("SQ", [128, 2, 1024], BF16))
    c.DELTA = e(nc.sbuf_tensor("DELTA", [128, 2, 1024], F32))
    c.ROWS = e(nc.sbuf_tensor("ROWS", [1, 3072], F32))
    c.ONESB = e(nc.sbuf_tensor("ONESB", [1, 128], BF16))
    c.ONESF = e(nc.sbuf_tensor("ONESF", [1, 128], F32))
    c.ONESC = e(nc.sbuf_tensor("ONESC", [128, 1], BF16))
    c.BIAS = e(nc.sbuf_tensor("BIAS", [128, 176], F32))
    c.KH = e(nc.sbuf_tensor("KH", [128, 2, 2048], BF16))
    c.VTH = e(nc.sbuf_tensor("VTH", [128, 2, 16, 128], BF16))
    c.PT = e(nc.sbuf_tensor("PT", [128, 2, 1024], BF16))
    c.ONESN = e(nc.sbuf_tensor("ONESN", [1, 128], F32))
    c.MASK = e(nc.sbuf_tensor("MASK", [128, 256], BF16))
    c.AMASK = e(nc.sbuf_tensor("AMASK", [128, 256], F32))
    c.ACOL = e(nc.sbuf_tensor("ACOL", [128, 256], F32))
    c.BVR = e(nc.sbuf_tensor("BVR", [1, 1024], BF16))
    c.STG = e(nc.sbuf_tensor("STG", [128, 2, 1024], BF16))
    c.LG = e(nc.sbuf_tensor("LG", [128, 64], F32))
    c.MRT = e(nc.sbuf_tensor("MRT", [128, 7, 8], F32))
    c.W64 = e(nc.sbuf_tensor("W64", [128, 64], F32))
    c.WRTS = e(nc.sbuf_tensor("WRTS", [128, 128], BF16))
    c.SGB = e(nc.sbuf_tensor("SGB", [128, 1024], BF16))
    c.HB = e(nc.sbuf_tensor("HB", [128, 1024], BF16))
    c.IDF = e(nc.sbuf_tensor("IDF", [128, 128], F32))

    c.PS = [e(nc.psum_tensor(f"P{i}", [128, 1024], F32)) for i in range(4)]
    c.pstrk = [Trk() for _ in range(4)]

    for nm in ["sWL", "sIO", "sPE", "sACT", "sDVE", "sGP", "sCC", "sCC2",
               "sCC3", "sCC4", "sCC5", "sOUT"]:
        setattr(c, nm, Sem(e(nc.semaphore(nm))))

    c.xn_trk = [Trk() for _ in range(DC)]
    c.sq_trk = [Trk() for _ in range(2)]
    c.ws_trk = [Trk() for _ in range(4)]
    c.dl_trk = [Trk() for _ in range(2)]
    c.ws_next = 0
    c.r_tok = None
    c.misc = {}

    c.block = e(nc.Block())

    emit_prologue(c, first)
    for p in phases:
        if p == "pre0" or p == "post0":
            emit_ffn0(c)
        elif p == "attn0":
            emit_attn(c, 0)
        elif p == "attn1":
            emit_attn(c, 1)
        elif p == "moe":
            emit_moe(c)
    emit_epilogue(c)
    stk.close()
    return nc, c.inputs_needed



def emit_prologue(c, first):
    blk = c.block
    din = c.din
    need_f = "pre0" in c.phases or "post0" in c.phases

    if need_f:
        @blk.gpsimd
        def _(g):
            rview = c.R[:].rearrange("p k m -> p (k m)")[:, 0:4096] \
                        .rearrange("p (q m) -> p q m", m=2048)
            last = None
            for r in range(8):
                if last is not None:
                    waits(g, [last])
                src = c.din["fac"][r].rearrange("(q p) m -> p q m", p=128)
                if r == 0:
                    i = g.dma_start(out=rview, in_=src)
                else:
                    i = g.dma_start(out=rview, in_=src, accum_op=ALU.add)
                last = c.sWL.tok(i, 16)
            i = g.dma_start(out=c.fslice.ap().rearrange("(q p) m -> p q m", p=128),
                            in_=rview)
            fst = c.sWL.tok(i, 16)
            c.misc["fslice_done"] = fst
            waits(g, [fst])
            i = g.collective_compute(
                "AllGather", ALU.bypass, replica_groups=[list(range(8))],
                ins=[c.fslice.ap().opt()], outs=[c.fgath.ap().opt()])
            c.misc["f_ready"] = c.sCC.tok(i)

    @blk.sync
    def _(sy):
        if need_f:
            waits(sy, [c.misc["fslice_done"]])
        src = din["xs"] if first else din["r_in"]
        i = sy.dma_start(out=c.R[:], in_=src.rearrange("(k p) m -> p k m", p=128))
        c.misc["rload"] = c.sIO.tok(i, 16)

    @blk.gpsimd
    def _(g):
        g.memset(c.ONESB[:], 1.0)
        g.memset(c.ONESF[:], 1.0)
        g.memset(c.ONESC[:], 1.0)
        i = g.memset(c.ONESN[:], -1.0)
        c.misc["ones"] = c.sGP.tok(i)
        g.memset(c.IDF[:], 1.0)
        i = g.affine_select(c.IDF[:], c.IDF[:], [[1, 128]], ALU.is_equal, 0.0,
                            base=0, channel_multiplier=-1)
        c.misc["idf"] = c.sGP.tok(i)
        toks = []
        for nm, off in BOFF.items():
            if nm in din:
                n = IN_SHAPES[nm][0][1]
                i = g.dma_start(out=c.BIAS[:, off:off + n], in_=din[nm][:, :])
                toks.append(c.sWL.tok(i, 16))
        if "maskA" in din:
            i = g.dma_start(out=c.MASK[:, 0:128], in_=din["maskA"][:])
            toks.append(c.sWL.tok(i, 16))
            i = g.dma_start(out=c.MASK[:, 128:256], in_=din["maskB"][:])
            toks.append(c.sWL.tok(i, 16))
        if "acol" in din:
            i = g.dma_start(out=c.ACOL[:], in_=din["acol"][:])
            toks.append(c.sWL.tok(i, 16))
        if "amaskA" in din:
            i = g.dma_start(out=c.AMASK[:, 0:128], in_=din["amaskA"][:])
            toks.append(c.sWL.tok(i, 16))
            i = g.dma_start(out=c.AMASK[:, 128:256], in_=din["amaskB"][:])
            toks.append(c.sWL.tok(i, 16))
        c.misc["bias_toks"] = toks

    @blk.vector
    def _(v):
        waits(v, c.misc["bias_toks"])
        t = None
        if "bffn0" in din:
            i = v.tensor_scalar(c.BIAS[:, DOFF["hbffn0"]:DOFF["hbffn0"] + 16],
                                c.BIAS[:, BOFF["bffn0"]:BOFF["bffn0"] + 16],
                                0.5, None, ALU.mult)
            t = c.sDVE.tok(i)
        if "bq" in din:
            i = v.tensor_scalar(c.BIAS[:, DOFF["sbq"]:DOFF["sbq"] + 16],
                                c.BIAS[:, BOFF["bq"]:BOFF["bq"] + 16],
                                QSCALE, None, ALU.mult)
            t = c.sDVE.tok(i)
        c.misc["dbias"] = t



BOFF = {"bff1": 0, "bff2": 64, "bffn0": 80, "bq": 96, "bk": 112, "bo": 128}
DOFF = {"hbffn0": 144, "sbq": 160}



def emit_epilogue(c):
    blk = c.block

    @blk.sync
    def _(sy):
        waits(sy, [c.r_tok, c.misc.get("dbg0"), c.misc.get("dbg1"), c.misc.get("dbgh"), c.misc.get("dbgr"), c.misc.get("dbgw"), c.misc.get("dbgrs"), c.misc.get("dbg2"), c.misc.get("dbg3")])
        i = sy.dma_start(out=c.r_out.rearrange("(k p) m -> p k m", p=128), in_=c.R[:])
        t = c.sOUT.tok(i, 16)
        waits(sy, [t])



def emit_ln(c, add_n1=False):
    blk = c.block
    PA, PB, PC, PD = c.PS
    tA, tB, tC, tD = c.pstrk

    cast_toks = []

    @blk.vector
    def _(v):
        waits(v, [c.misc.get("rload"), c.r_tok])
        for d in range(DC):
            c.xn_trk[d].wr(v)
            i = v.tensor_copy(c.XN[:, d, :], c.R[:, d, :])
            tk = c.sDVE.tok(i)
            cast_toks.append(tk)
            c.xn_trk[d].wdone(tk)

    sq_toks = []

    @blk.scalar
    def _(s):
        for d in range(DC):
            waits(s, [cast_toks[d]])
            c.sq_trk[d % 2].wr(s)
            i = s.activation(c.SQ[:, d % 2, :], c.XN[:, d, :], AF.Square)
            sq_toks.append(c.sACT.tok(s.drain()))

    @blk.tensor
    def _(t):
        tC.wr(t)
        tD.wr(t)
        waits(t, [c.misc["ones"]])
        mm_sum = mm_sq = None
        tok_sum = tok_sq = None
        for d in range(DC):
            waits(t, [cast_toks[d]])
            for h0 in range(2):
                sl = slice(h0 * 512, (h0 + 1) * 512)
                mm_sum = t.matmul(PC[0:1, sl], c.ONESC[:], c.XN[:, d, sl],
                                  start=(d == 0), stop=(d == DC - 1))
            if d == DC - 1:
                tok_sum = c.sPE.tok(mm_sum)
            waits(t, [sq_toks[d]])
            for h0 in range(2):
                sl = slice(h0 * 512, (h0 + 1) * 512)
                mm_sq = t.matmul(PD[0:1, sl], c.ONESC[:], c.SQ[:, d % 2, sl],
                                 start=(d == 0), stop=(d == DC - 1))
            tok_sq = c.sPE.tok(mm_sq)
            c.sq_trk[d % 2].rdone(tok_sq)
        tC.wdone(tok_sum)
        tD.wdone(tok_sq)

    MEAN, RSTD, SCR = slice(0, 1024), slice(1024, 2048), slice(2048, 3072)
    MR = MEAN
    row_tok = []

    @blk.vector
    def _(v):
        tC.rd(v)
        tD.rd(v)
        i = v.tensor_scalar(c.ROWS[0:1, MEAN], PC[0:1, :], 1.0 / 2048.0, None, ALU.mult)
        tC.rdone(c.sDVE.tok(i))
        i = v.tensor_scalar(c.ROWS[0:1, SCR], PD[0:1, :], 1.0 / 2048.0, 1e-5,
                            ALU.mult, op1=ALU.add)
        tD.rdone(c.sDVE.tok(i))
        i = v.tensor_tensor(c.ROWS[0:1, RSTD], c.ROWS[0:1, MEAN], c.ROWS[0:1, MEAN],
                            ALU.mult)
        i = v.tensor_tensor(c.ROWS[0:1, SCR], c.ROWS[0:1, SCR], c.ROWS[0:1, RSTD],
                            ALU.subtract)
        row_tok.append(c.sDVE.tok(i))

    @blk.scalar
    def _(s):
        waits(s, row_tok)
        i = s.activation(c.ROWS[0:1, SCR], c.ROWS[0:1, SCR], AF.Ln, bias=0.0, scale=1.0)
        i = s.activation(c.ROWS[0:1, RSTD], c.ROWS[0:1, SCR], AF.Exp, bias=0.0, scale=-0.5)
        row_tok.append(c.sACT.tok(s.drain()))

    @blk.vector
    def _(v):
        waits(v, row_tok)
        i = v.tensor_tensor(c.ROWS[0:1, MR], c.ROWS[0:1, MEAN], c.ROWS[0:1, RSTD],
                            ALU.mult)
        row_tok.append(c.sDVE.tok(i))

    @blk.tensor
    def _(t):
        tA.wr(t)
        tB.wr(t)
        waits(t, row_tok)
        for h0 in range(2):
            sl = slice(h0 * 512, (h0 + 1) * 512)
            ia = t.matmul(PA[:, sl], c.ONESF[:], c.ROWS[0:1, 1024 + h0 * 512:1024 + (h0 + 1) * 512],
                          start=True, stop=True)
            ib = t.matmul(PB[:, sl], c.ONESF[:], c.ROWS[0:1, h0 * 512:(h0 + 1) * 512],
                          start=True, stop=True)
        tA.wdone(c.sPE.tok(ia))
        tB.wdone(c.sPE.tok(ib))

    @blk.vector
    def _(v):
        tA.rd(v)
        tB.rd(v)
        c.dl_trk[1].wr(v)
        if add_n1:
            c.dl_trk[0].wr(v)
        tmp = c.DELTA[:, 1, :]
        i = None
        for d in range(DC):
            v.tensor_tensor(tmp, c.R[:, d, :], PA[:], ALU.mult)
            if not add_n1:
                i = v.tensor_tensor(c.XN[:, d, :], tmp, PB[:], ALU.subtract)
            else:
                t2 = c.DELTA[:, 0, :]
                v.tensor_tensor(t2, tmp, PB[:], ALU.subtract)
                v.tensor_tensor(c.R[:, d, :], c.R[:, d, :], t2, ALU.add)
                i = v.tensor_copy(c.XN[:, d, :], t2)
            tk = c.sDVE.tok(i)
            c.xn_trk[d].wdone(tk)
        tA.rdone((c.sDVE, c.sDVE.n))
        tB.rdone((c.sDVE, c.sDVE.n))
        c.dl_trk[1].wdone((c.sDVE, c.sDVE.n))
        if add_n1:
            c.r_tok = (c.sDVE, c.sDVE.n)
            c.dl_trk[0].wdone((c.sDVE, c.sDVE.n))



def ws_issue(c, g, src_ap, nelem):
    slot = c.ws_next
    c.ws_next = (c.ws_next + 1) % 4
    trk = c.ws_trk[slot]
    trk.wr(g)
    dst = c.WS[:, slot, 0:nelem]
    shp = src_ap.shape
    if len(shp) == 3:
        dst = dst.rearrange("p (k m) -> p k m", m=shp[2])
    i = g.dma_start(out=dst, in_=src_ap)
    tk = c.sWL.tok(i, 16)
    trk.wdone(tk)
    return slot


def ws_view(c, slot, k, m):
    return c.WS[:, slot, 0:k * m].rearrange("p (k m) -> p k m", m=m)



def emit_ffn0(c):
    emit_ln(c, add_n1=False)
    blk = c.block
    PA, PB = c.PS[0], c.PS[1]
    tA, tB = c.pstrk[0], c.pstrk[1]
    fg = c.fgath.ap()

    GM = 1
    for g0 in range(0, DC, GM):
        slotbox = []

        @blk.gpsimd
        def _(g, g0=g0):
            if g0 == 0:
                waits(g, [c.misc["f_ready"]])
            src = fg[:, g0 * 128:(g0 + GM) * 128].rearrange("(k p) m -> p k m", p=128)
            slotbox.append(ws_issue(c, g, src, 16 * GM * 128))

        for mi in range(GM):
            m = g0 + mi
            ps, trk = (PA, tA) if m % 2 == 0 else (PB, tB)

            @blk.tensor
            def _(t, m=m, mi=mi, ps=ps, trk=trk, slotbox=slotbox):
                slot = slotbox[0]
                trk.wr(t)
                c.ws_trk[slot].rd(t)
                waits(t, [c.xn_trk[k].w for k in range(DC)])
                lhsv = ws_view(c, slot, 16, GM * 128)
                last = None
                for k in range(DC):
                    lh = lhsv[:, k, mi * 128:(mi + 1) * 128]
                    for h0 in range(2):
                        sl = slice(h0 * 512, (h0 + 1) * 512)
                        last = t.matmul(ps[:, sl], lh, c.XN[:, k, sl],
                                        start=(k == 0), stop=(k == DC - 1))
                tk = c.sPE.tok(last)
                trk.wdone(tk)
                if mi == GM - 1:
                    c.ws_trk[slot].rdone(tk)
                if m == DC - 1:
                    for d in range(DC):
                        c.xn_trk[d].rdone(tk)

            @blk.scalar
            def _(s, m=m, ps=ps, trk=trk):
                trk.rd(s)
                c.dl_trk[m % 2].wr(s)
                hb = c.BIAS[:, DOFF["hbffn0"] + m:DOFF["hbffn0"] + m + 1]
                waits(s, [c.misc.get("dbias")])
                i = s.activation(c.DELTA[:, m % 2, :], ps[:], AF.Identity,
                                 bias=hb, scale=C1)
                tk = c.sACT.tok(s.drain())
                trk.rdone(tk)
                c.dl_trk[m % 2].wdone(tk)

            @blk.vector
            def _(v, m=m):
                c.dl_trk[m % 2].rd(v)
                i = v.tensor_tensor(c.R[:, m, :], c.R[:, m, :], c.DELTA[:, m % 2, :],
                                    ALU.add)
                c.r_tok = c.sDVE.tok(i)
                c.dl_trk[m % 2].rdone(c.r_tok)



def emit_attn(c, L):
    blk = c.block
    PA, PB, PC, PD = c.PS
    tA, tB, tC, tD = c.pstrk
    kvcon = getattr(c, f"kvcon{L}").ap()
    kvg = getattr(c, f"kvg{L}").ap()
    agsem = c.sCC2 if L == 0 else c.sCC4
    din = c.din

    if not hasattr(c, "biga_trk"):
        c.biga_trk = [Trk() for _ in range(16)]
        c.stg_trk = [Trk() for _ in range(2)]
        c.kh_trk = [Trk() for _ in range(2)]
        c.vth_trk = [Trk() for _ in range(2)]
        c.crb_trk = Trk()

    emit_ln(c, add_n1=(L == 0))

    kproj_toks = []
    for m in range(DC):
        slotbox = []

        @blk.gpsimd
        def _(g, m=m, slotbox=slotbox):
            if L == 0:
                src = din["wqkv"][:, 2048 + m * 128:2048 + (m + 1) * 128]
            else:
                src = din["wk"][:, m * 128:(m + 1) * 128]
            slotbox.append(ws_issue(c, g, src.rearrange("(k p) m -> p k m", p=128),
                                    16 * 128))

        ps, trk = (PA, tA) if m % 2 == 0 else (PB, tB)

        @blk.tensor
        def _(t, m=m, ps=ps, trk=trk, slotbox=slotbox):
            slot = slotbox[0]
            trk.wr(t)
            c.ws_trk[slot].rd(t)
            waits(t, [c.xn_trk[k].w for k in range(DC)])
            lhsv = ws_view(c, slot, 16, 128)
            last = None
            for k in range(DC):
                for h0 in range(2):
                    sl = slice(h0 * 512, (h0 + 1) * 512)
                    last = t.matmul(ps[:, sl], lhsv[:, k, :], c.XN[:, k, sl],
                                    start=(k == 0), stop=(k == DC - 1))
            tk = c.sPE.tok(last)
            trk.wdone(tk)
            c.ws_trk[slot].rdone(tk)

        @blk.scalar
        def _(s, m=m, ps=ps, trk=trk):
            trk.rd(s)
            c.stg_trk[m % 2].wr(s)
            if L == 0:
                i = s.activation(c.STG[:, m % 2, :], ps[:], AF.Identity,
                                 bias=0.0, scale=1.0)
            else:
                i = s.activation(c.STG[:, m % 2, :], ps[:], AF.Identity,
                                 bias=c.BIAS[:, BOFF["bk"] + m:BOFF["bk"] + m + 1],
                                 scale=1.0)
            tk = c.sACT.tok(s.drain())
            trk.rdone(tk)
            c.stg_trk[m % 2].wdone(tk)

        @blk.sync
        def _(sy, m=m):
            c.stg_trk[m % 2].rd(sy)
            i = sy.dma_start(out=kvcon[m * 128:(m + 1) * 128, :], in_=c.STG[:, m % 2, :])
            tk = c.sIO.tok(i, 16)
            c.stg_trk[m % 2].rdone(tk)
            kproj_toks.append(tk)

    vproj_toks = []
    if not hasattr(c, "bvr_trk"):
        c.bvr_trk = Trk()
    for half in range(2):
        @blk.gpsimd
        def _(g, half=half):
            if L == 1:
                c.bvr_trk.wr(g)
                i = g.dma_start(out=c.BVR[:],
                                in_=din["bvrow"][0:1, half * 1024:(half + 1) * 1024])
                c.bvr_trk.wdone(c.sWL.tok(i, 16))
            for d in range(16):
                c.biga_trk[d].wr(g)
            if L == 0:
                src = din["wqkv"][:, 4096 + half * 1024:4096 + (half + 1) * 1024]
            else:
                src = din["wv"][:, half * 1024:(half + 1) * 1024]
            i = g.dma_start(out=c.BIGA[:],
                            in_=src.rearrange("(k p) m -> p k m", p=128))
            tk = c.sWL.tok(i, 16)
            for d in range(16):
                c.biga_trk[d].wdone(tk)

        for tt in range(NT):
            ps, trk = (PA, tA) if tt % 2 == 0 else (PB, tB)

            @blk.tensor
            def _(t, half=half, tt=tt, ps=ps, trk=trk):
                trk.wr(t)
                for d in range(16):
                    c.biga_trk[d].rd(t)
                waits(t, [c.xn_trk[k].w for k in range(DC)] + [c.misc["ones"]])
                last = None
                for k in range(DC):
                    lh = c.XN[:, k, tt * 128:(tt + 1) * 128]
                    laststep = (k == DC - 1) and (L == 0)
                    for h0 in range(2):
                        sl = slice(h0 * 512, (h0 + 1) * 512)
                        last = t.matmul(ps[:, sl], lh, c.BIGA[:, k, sl],
                                        start=(k == 0), stop=laststep)
                if L == 1:
                    c.bvr_trk.rd(t)
                    for h0 in range(2):
                        sl = slice(h0 * 512, (h0 + 1) * 512)
                        last = t.matmul(ps[:, sl], c.ONESB[:], c.BVR[0:1, sl],
                                        start=False, stop=True)
                tk = c.sPE.tok(last)
                if L == 1 and tt == NT - 1:
                    c.bvr_trk.rdone(tk)
                trk.wdone(tk)
                for d in range(16):
                    c.biga_trk[d].rdone(tk)

            @blk.scalar
            def _(s, half=half, tt=tt, ps=ps, trk=trk):
                trk.rd(s)
                c.stg_trk[tt % 2].wr(s)
                i = s.activation(c.STG[:, tt % 2, :], ps[:], AF.Identity,
                                 bias=0.0, scale=1.0)
                tk = c.sACT.tok(s.drain())
                trk.rdone(tk)
                c.stg_trk[tt % 2].wdone(tk)

            @blk.sync
            def _(sy, half=half, tt=tt):
                c.stg_trk[tt % 2].rd(sy)
                vdst = kvcon[2048:4096, :].rearrange("(t two) m -> t two m", two=2)
                i = sy.dma_start(out=vdst[tt * 128:(tt + 1) * 128, half, :],
                                 in_=c.STG[:, tt % 2, :])
                tk = c.sIO.tok(i, 16)
                c.stg_trk[tt % 2].rdone(tk)
                vproj_toks.append(tk)

    agbox = []

    def kick_ag():
        @blk.gpsimd
        def _(g):
            waits(g, kproj_toks + vproj_toks)
            i = g.collective_compute(
                "AllGather", ALU.bypass,
                replica_groups=[[0, 1], [2, 3], [4, 5], [6, 7]],
                ins=[kvcon.opt()], outs=[kvg.opt()])
            agbox.append(agsem.tok(i))

    if AG_OVERLAP:
        kick_ag()
    if L == 0:
        emit_ff_l0(c)

    if DEBUG_ATTN and L == 0:
        @blk.sync
        def _(sy):
            waits(sy, [c.r_tok])
            i = sy.dma_start(out=c.dbg_rff.rearrange("(k p) m -> p k m", p=128), in_=c.R[:])
            c.misc["dbgr"] = c.sIO.tok(i, 16)

    for m in range(DC):
        slotbox = []

        @blk.gpsimd
        def _(g, m=m, slotbox=slotbox):
            if L == 0:
                src = din["wqkv"][:, m * 128:(m + 1) * 128]
            else:
                src = din["wq"][:, m * 128:(m + 1) * 128]
            slotbox.append(ws_issue(c, g, src.rearrange("(k p) m -> p k m", p=128),
                                    16 * 128))

        ps, trk = (PA, tA) if m % 2 == 0 else (PB, tB)

        @blk.tensor
        def _(t, m=m, ps=ps, trk=trk, slotbox=slotbox):
            slot = slotbox[0]
            trk.wr(t)
            c.ws_trk[slot].rd(t)
            waits(t, [c.xn_trk[k].w for k in range(DC)])
            lhsv = ws_view(c, slot, 16, 128)
            last = None
            for k in range(DC):
                for h0 in range(2):
                    sl = slice(h0 * 512, (h0 + 1) * 512)
                    last = t.matmul(ps[:, sl], lhsv[:, k, :], c.XN[:, k, sl],
                                    start=(k == 0), stop=(k == DC - 1))
            tk = c.sPE.tok(last)
            trk.wdone(tk)
            c.ws_trk[slot].rdone(tk)
            if m == DC - 1:
                for d in range(DC):
                    c.xn_trk[d].rdone(tk)

        @blk.scalar
        def _(s, m=m, ps=ps, trk=trk):
            trk.rd(s)
            c.biga_trk[m].wr(s)
            if L == 1:
                i = s.activation(c.BIGA[:, m, :], ps[:], AF.Identity,
                                 bias=c.BIAS[:, DOFF["sbq"] + m:DOFF["sbq"] + m + 1],
                                 scale=QSCALE)
            else:
                i = s.activation(c.BIGA[:, m, :], ps[:], AF.Identity,
                                 bias=0.0, scale=QSCALE)
            tk = c.sACT.tok(s.drain())
            trk.rdone(tk)
            c.biga_trk[m].wdone(tk)


    if not AG_OVERLAP:
        kick_ag()

    if DEBUG_ATTN and L == DBG_L:
        @blk.gpsimd
        def _(g):
            waits(g, kproj_toks + vproj_toks)
            i = g.dma_start(out=c.dbg_kvc, in_=kvcon)
            c.misc["dbg0"] = c.sWL.tok(i, 16)
            waits(g, agbox)
            i = g.dma_start(out=c.dbg_kvg, in_=kvg)
            c.misc["dbg1"] = c.sWL.tok(i, 16)
            waits(g, [c.biga_trk[m].w for m in range(16)])
            i = g.dma_start(out=c.dbg_q.rearrange("(k p) m -> p k m", p=128), in_=c.BIGA[:])
            tk = c.sWL.tok(i, 16)
            c.misc["dbg2"] = tk
            for m in range(16):
                c.biga_trk[m].rdone(tk)

    arg_trk = [Trk(), Trk()]
    for si_ in range(2):
        arg_trk[si_].w = c.dl_trk[si_].w
        arg_trk[si_].r = list(c.dl_trk[si_].r)
    pah_trk = [Trk(), Trk()]
    pah_trk[0].w = tA.w
    pah_trk[0].r = list(tA.r)
    pah_trk[1].w = tD.w
    pah_trk[1].r = list(tD.r)
    pslot_ctr = [0]
    ROWLANE = slice(2048, 3072)

    for h in range(H):
        @blk.sync
        def _(sy, h=h):
            c.kh_trk[h % 2].wr(sy)
            waits(sy, agbox)
            dst = c.KH[:, h % 2, :].rearrange("p (l two f) -> p l two f", two=2, f=128)
            i = sy.dma_start(out=dst[:, :, 1, :],
                             in_=kvg[h * 128:(h + 1) * 128, :]
                             .rearrange("p (l f) -> p l f", f=128))
            c.sIO.tok(i, 16)
            i = sy.dma_start(out=dst[:, :, 0, :],
                             in_=kvg[4096 + h * 128:4096 + (h + 1) * 128, :]
                             .rearrange("p (l f) -> p l f", f=128))
            c.kh_trk[h % 2].wdone(c.sIO.tok(i, 16))
            c.vth_trk[h % 2].wr(sy)
            vdst = c.VTH[:, h % 2].rearrange("p (l two) f -> p l two f", two=2)
            vsrc0 = kvg[2048:4096, :].rearrange(
                "(l p two) m -> p l (two m)", p=128, two=2)[:, :, (h // 8) * 1024 + (h % 8) * 128:
                                                           (h // 8) * 1024 + (h % 8) * 128 + 128]
            i = sy.dma_start(out=vdst[:, :, 1, :], in_=vsrc0)
            c.sIO.tok(i, 16)
            vsrc1 = kvg[4096 + 2048:8192, :].rearrange(
                "(l p two) m -> p l (two m)", p=128, two=2)[:, :, (h // 8) * 1024 + (h % 8) * 128:
                                                           (h // 8) * 1024 + (h % 8) * 128 + 128]
            i = sy.dma_start(out=vdst[:, :, 0, :], in_=vsrc1)
            c.vth_trk[h % 2].wdone(c.sIO.tok(i, 16))

        if L == 1:
            @blk.sync
            def _(sy, h=h):
                waits(sy, [c.misc.get("lane_free")])
                i = sy.dma_start(out=c.ROWS[0:1, ROWLANE],
                                 in_=din["crow"][0:1, h * 1024:(h + 1) * 1024])
                c.misc["lane_tok"] = c.sIO.tok(i, 16)

        for kc in range(16):
            q0 = (kc // 2) * 128
            n = 1024 - q0
            slot = kc % 2
            ps = PA if slot == 0 else PD

            @blk.tensor
            def _(t, h=h, kc=kc, q0=q0, slot=slot, ps=ps):
                pah_trk[slot].wr(t)
                c.kh_trk[h % 2].rd(t)
                c.biga_trk[h].rd(t)
                if L == 1:
                    waits(t, [c.misc["lane_tok"]])
                lh = c.KH[:, h % 2, kc * 128:(kc + 1) * 128]
                if q0 < 512:
                    t.matmul(ps[:, q0:512], lh, c.BIGA[:, h, q0:512],
                             start=True, stop=(L == 0))
                    i = t.matmul(ps[:, 512:1024], lh, c.BIGA[:, h, 512:1024],
                                 start=True, stop=(L == 0))
                    if L == 1:
                        t.matmul(ps[:, q0:512], c.ONESN[:],
                                 c.ROWS[0:1, 2048 + q0:2048 + 512],
                                 start=False, stop=True, skip_group_check=True)
                        i = t.matmul(ps[:, 512:1024], c.ONESN[:],
                                     c.ROWS[0:1, 2048 + 512:2048 + 1024],
                                     start=False, stop=True, skip_group_check=True)
                else:
                    i = t.matmul(ps[:, q0:1024], lh, c.BIGA[:, h, q0:1024],
                                 start=True, stop=(L == 0))
                    if L == 1:
                        i = t.matmul(ps[:, q0:1024], c.ONESN[:],
                                     c.ROWS[0:1, 2048 + q0:2048 + 1024],
                                     start=False, stop=True, skip_group_check=True)
                tk = c.sPE.tok(i)
                pah_trk[slot].wdone(tk)
                if kc == 15:
                    c.kh_trk[h % 2].rdone(tk)
                    if L == 1:
                        c.misc["lane_free"] = tk

            if L == 1:
                @blk.vector
                def _(v, h=h, kc=kc, q0=q0, slot=slot, ps=ps):
                    pah_trk[slot].rd(v)
                    msl = slice((kc % 2) * 128, (kc % 2) * 128 + 128)
                    i = v.tensor_tensor(ps[:, q0:q0 + 128], ps[:, q0:q0 + 128],
                                        c.AMASK[:, msl], ALU.add)
                    tk = c.sDVE.tok(i)
                    pah_trk[slot].wdone(tk)

            @blk.scalar
            def _(s, h=h, kc=kc, q0=q0, slot=slot, ps=ps):
                c.pt_trk = getattr(c, "pt_trk", [Trk(), Trk()])
                c.pt_trk[slot].wr(s)
                pah_trk[slot].rd(s)
                if L == 1:
                    aco = c.ACOL[:, h * 16 + kc:h * 16 + kc + 1]
                    i = s.activation(c.PT[:, slot, q0:1024], ps[:, q0:1024],
                                     AF.Exp, bias=aco, scale=1.0)
                else:
                    i = s.activation(c.PT[:, slot, q0:1024], ps[:, q0:1024],
                                     AF.Exp, bias=0.0, scale=1.0)
                tk = c.sACT.tok(i)
                pah_trk[slot].rdone(tk)
                c.pt_trk[slot].wdone(tk)

            if L == 0:
                @blk.vector
                def _(v, kc=kc, q0=q0, slot=slot):
                    c.pt_trk[slot].rd(v)
                    msl = slice((kc % 2) * 128, (kc % 2) * 128 + 128)
                    i = v.tensor_tensor(c.PT[:, slot, q0:q0 + 128],
                                        c.PT[:, slot, q0:q0 + 128],
                                        c.MASK[:, msl], ALU.mult)
                    c.pt_trk[slot].wdone(c.sDVE.tok(i))

            @blk.tensor
            def _(t, h=h, kc=kc, q0=q0, slot=slot):
                c.pt_trk[slot].rd(t)
                c.vth_trk[h % 2].rd(t)
                if kc == 0:
                    tB.wr(t)
                    tC.wr(t)
                pieces = [(q0, 512), (512, 1024)] if q0 < 512 else [(q0, 1024)]
                i = None
                for (a, b) in pieces:
                    t.matmul(PB[:, a:b], c.VTH[:, h % 2, kc, :],
                             c.PT[:, slot, a:b],
                             start=(kc == 0), stop=False, skip_group_check=True)
                    i = t.matmul(PC[0:1, a:b], c.ONESC[:], c.PT[:, slot, a:b],
                                 start=(kc == 0), stop=False, skip_group_check=True)
                tk = c.sPE.tok(i)
                c.pt_trk[slot].rdone(tk)
                c.misc["last_avsum"] = tk

        @blk.tensor
        def _(t, h=h):
            pass

        @blk.vector
        def _(v, h=h):
            waits(v, [c.misc["last_avsum"]])
            i = v.reciprocal(c.ROWS[0:1, ROWLANE], PC[0:1, :])
            tk = c.sDVE.tok(i)
            tC.rdone(tk)
            c.misc["recip_tok"] = tk

        @blk.tensor
        def _(t, h=h):
            pah_trk[1].wr(t)
            waits(t, [c.misc["recip_tok"]])
            for h0 in range(2):
                sl = slice(h0 * 512, (h0 + 1) * 512)
                i = t.matmul(PD[:, sl], c.ONESF[:],
                             c.ROWS[0:1, 2048 + h0 * 512:2048 + (h0 + 1) * 512],
                             start=True, stop=True)
            tk = c.sPE.tok(i)
            pah_trk[1].wdone(tk)
            c.misc["lane_free"] = tk

        @blk.scalar
        def _(s, h=h):
            pah_trk[1].rd(s)
            arg_trk[0].wr(s)
            i = s.activation(c.DELTA[:, 0, :], PD[:], AF.Identity, bias=0.0, scale=1.0)
            tk = c.sACT.tok(s.drain())
            pah_trk[1].rdone(tk)
            arg_trk[0].wdone(tk)

        @blk.vector
        def _(v, h=h):
            arg_trk[0].rd(v)
            c.biga_trk[h].wr(v)
            i = v.tensor_tensor(c.BIGA[:, h, :], PB[:], c.DELTA[:, 0, :], ALU.mult)
            tk = c.sDVE.tok(i)
            tB.rdone(tk)
            arg_trk[0].rdone(tk)
            c.biga_trk[h].wdone(tk)
            c.vth_trk[h % 2].rdone(tk)

    if DEBUG_ATTN and L == DBG_L:
        @blk.gpsimd
        def _(g):
            waits(g, [c.biga_trk[m].w for m in range(16)])
            i = g.dma_start(out=c.dbg_ctx.rearrange("(k p) m -> p k m", p=128), in_=c.BIGA[:])
            tk = c.sWL.tok(i, 16)
            c.misc["dbg3"] = tk
            for m in range(16):
                c.biga_trk[m].rdone(tk)

    tA.r = tA.r + pah_trk[0].r + [t for t in (pah_trk[0].w,) if t]
    tD.r = tD.r + pah_trk[1].r + [t for t in (pah_trk[1].w,) if t]
    for si_ in range(2):
        c.dl_trk[si_].r = c.dl_trk[si_].r + arg_trk[si_].r + \
            [t for t in (arg_trk[si_].w,) if t]

    for m in range(DC):
        slotbox = []

        @blk.gpsimd
        def _(g, m=m, slotbox=slotbox):
            w = din["wproj"] if L == 0 else din["wo"]
            src = w[:, m * 128:(m + 1) * 128]
            slotbox.append(ws_issue(c, g, src.rearrange("(k p) m -> p k m", p=128),
                                    16 * 128))

        ps, trk = (PA, tA) if m % 2 == 0 else (PB, tB)

        @blk.tensor
        def _(t, m=m, ps=ps, trk=trk, slotbox=slotbox):
            slot = slotbox[0]
            trk.wr(t)
            c.ws_trk[slot].rd(t)
            for d in range(DC):
                c.biga_trk[d].rd(t)
            lhsv = ws_view(c, slot, 16, 128)
            last = None
            for k in range(DC):
                for h0 in range(2):
                    sl = slice(h0 * 512, (h0 + 1) * 512)
                    last = t.matmul(ps[:, sl], lhsv[:, k, :], c.BIGA[:, k, sl],
                                    start=(k == 0), stop=(k == DC - 1))
            tk = c.sPE.tok(last)
            trk.wdone(tk)
            c.ws_trk[slot].rdone(tk)
            if m == DC - 1:
                for d in range(DC):
                    c.biga_trk[d].rdone(tk)

        @blk.vector
        def _(v, m=m, ps=ps, trk=trk):
            trk.rd(v)
            if L == 1:
                bo = c.BIAS[:, BOFF["bo"] + m:BOFF["bo"] + m + 1]
                i = v.scalar_tensor_tensor(c.R[:, m, :], ps[:], bo, c.R[:, m, :],
                                           ALU.add, ALU.add)
            else:
                i = v.tensor_tensor(c.R[:, m, :], c.R[:, m, :], ps[:], ALU.add)
            tk = c.sDVE.tok(i)
            trk.rdone(tk)
            c.r_tok = tk


def emit_ff_l0(c):
    blk = c.block
    PA, PB, PC, PD = c.PS
    tA, tB, tC, tD = c.pstrk
    din = c.din

    for grp in range(8):
        gbase = grp % 2 * 8
        for hc8 in range(8):
            hc = grp * 8 + hc8
            slotbox = []

            @blk.gpsimd
            def _(g, hc=hc, slotbox=slotbox):
                src = din["wff1"][:, hc * 128:(hc + 1) * 128]
                slotbox.append(ws_issue(c, g, src.rearrange("(k p) m -> p k m", p=128),
                                        16 * 128))

            ps, trk = (PA, tA) if hc % 2 == 0 else (PB, tB)

            @blk.tensor
            def _(t, hc=hc, ps=ps, trk=trk, slotbox=slotbox):
                slot = slotbox[0]
                trk.wr(t)
                c.ws_trk[slot].rd(t)
                waits(t, [c.xn_trk[k].w for k in range(DC)])
                lhsv = ws_view(c, slot, 16, 128)
                last = None
                for k in range(DC):
                    for h0 in range(2):
                        sl = slice(h0 * 512, (h0 + 1) * 512)
                        last = t.matmul(ps[:, sl], lhsv[:, k, :], c.XN[:, k, sl],
                                        start=(k == 0), stop=(k == DC - 1))
                tk = c.sPE.tok(last)
                trk.wdone(tk)
                c.ws_trk[slot].rdone(tk)

            @blk.scalar
            def _(s, hc=hc, gbase=gbase, hc8=hc8, ps=ps, trk=trk):
                trk.rd(s)
                c.biga_trk[gbase + hc8].wr(s)
                i = s.activation(c.BIGA[:, gbase + hc8, :], ps[:], AF.Gelu,
                                 bias=c.BIAS[:, BOFF["bff1"] + hc:BOFF["bff1"] + hc + 1],
                                 scale=1.0)
                tk = c.sACT.tok(s.drain())
                trk.rdone(tk)
                c.biga_trk[gbase + hc8].wdone(tk)

        if DEBUG_ATTN and grp == 0:
            @blk.gpsimd
            def _(g):
                waits(g, [c.biga_trk[j].w for j in range(8)])
                i = g.dma_start(out=c.dbg_h0.rearrange("(k p) m -> p k m", p=128),
                                in_=c.BIGA[:, 0:8, :])
                tk = c.sWL.tok(i, 16)
                c.misc["dbgh"] = tk
                for j in range(8):
                    c.biga_trk[j].rdone(tk)

        for m in range(DC):
            slotbox = []

            @blk.gpsimd
            def _(g, grp=grp, m=m, slotbox=slotbox):
                src = din["wff2"][grp * 1024:(grp + 1) * 1024, m * 128:(m + 1) * 128]
                slotbox.append(ws_issue(c, g, src.rearrange("(k p) m -> p k m", p=128),
                                        8 * 128))

            ps, trk = (PC, tC) if m % 2 == 0 else (PD, tD)

            @blk.tensor
            def _(t, grp=grp, m=m, gbase=gbase, ps=ps, trk=trk, slotbox=slotbox):
                slot = slotbox[0]
                trk.wr(t)
                c.ws_trk[slot].rd(t)
                for j in range(8):
                    c.biga_trk[gbase + j].rd(t)
                lhsv = ws_view(c, slot, 8, 128)
                last = None
                for j in range(8):
                    for h0 in range(2):
                        sl = slice(h0 * 512, (h0 + 1) * 512)
                        last = t.matmul(ps[:, sl], lhsv[:, j, :],
                                        c.BIGA[:, gbase + j, sl],
                                        start=(j == 0), stop=(j == 7))
                tk = c.sPE.tok(last)
                trk.wdone(tk)
                c.ws_trk[slot].rdone(tk)
                if m == DC - 1:
                    for j in range(8):
                        c.biga_trk[gbase + j].rdone(tk)

            @blk.vector
            def _(v, grp=grp, m=m, ps=ps, trk=trk):
                trk.rd(v)
                if grp == 0:
                    b2 = c.BIAS[:, BOFF["bff2"] + m:BOFF["bff2"] + m + 1]
                    i = v.scalar_tensor_tensor(c.R[:, m, :], ps[:], b2, c.R[:, m, :],
                                               ALU.add, ALU.add)
                else:
                    i = v.tensor_tensor(c.R[:, m, :], c.R[:, m, :], ps[:], ALU.add)
                tk = c.sDVE.tok(i)
                trk.rdone(tk)
                c.r_tok = tk



def emit_moe(c):
    blk = c.block
    PA, PB, PC, PD = c.PS
    tA, tB, tC, tD = c.pstrk
    din = c.din
    if not hasattr(c, "biga_trk"):
        c.biga_trk = [Trk() for _ in range(16)]

    emit_ln(c, add_n1=False)

    @blk.gpsimd
    def _(g):
        if not hasattr(c, "wrt_trk"):
            c.wrt_trk = Trk()
        c.wrt_trk.wr(g)
        i = g.dma_start(out=c.WRTS.rearrange("p (k e) -> p k e", e=8),
                        in_=din["wrt"].rearrange("(k p) e -> p k e", p=128))
        c.wrt_trk.wdone(c.sWL.tok(i, 16))

    @blk.tensor
    def _(t):
        tC.wr(t)
        c.wrt_trk.rd(t)
        waits(t, [c.xn_trk[k].w for k in range(DC)])
        last = None
        for tt in range(NT):
            for k in range(DC):
                last = t.matmul(PC[:, tt * 8:(tt + 1) * 8],
                                c.XN[:, k, tt * 128:(tt + 1) * 128],
                                c.WRTS[:, k * 8:(k + 1) * 8],
                                start=(k == 0), stop=(k == DC - 1),
                                skip_group_check=True)
        tk = c.sPE.tok(last)
        tC.wdone(tk)
        c.wrt_trk.rdone(tk)

    @blk.scalar
    def _(s):
        tC.rd(s)
        i = s.activation(c.LG[:], PC[:, 0:64], AF.Identity, bias=0.0, scale=1.0)
        tk = c.sACT.tok(s.drain())
        tC.rdone(tk)
        c.misc["rt0"] = tk

    D64 = c.DELTA[:, 1, 0:64]
    lg3 = c.LG.rearrange("p (t e) -> p t e", e=8)

    @blk.vector
    def _(v):
        waits(v, [c.misc["rt0"]])
        c.dl_trk[1].wr(v)
        i = v.tensor_reduce(c.MRT[:, 0, :], lg3, AX.X, ALU.max)
        waits(v, [c.sDVE.tok(i)])
        i = None
        for g8 in range(8):
            i = v.tensor_scalar(c.W64[:, g8 * 8:(g8 + 1) * 8], c.LG[:, g8 * 8:(g8 + 1) * 8],
                            c.MRT[:, 0, g8:g8 + 1], None, ALU.subtract)
        c.misc["rt1"] = c.sDVE.tok(i)

    @blk.scalar
    def _(s):
        waits(s, [c.misc["rt1"]])
        i = s.activation(c.LG[:], c.W64[:], AF.Exp, bias=0.0, scale=1.0)
        c.misc["rt2"] = c.sACT.tok(s.drain())

    @blk.vector
    def _(v):
        waits(v, [c.misc["rt2"]])
        i = v.tensor_reduce(c.MRT[:, 1, :], lg3, AX.X, ALU.add)
        c.misc["rt2b"] = c.sDVE.tok(i)

    @blk.scalar
    def _(s):
        waits(s, [c.misc["rt2b"]])
        s.activation(c.MRT[:, 5, :], c.MRT[:, 1, :], AF.Ln, bias=0.0, scale=1.0)
        i = s.activation(c.MRT[:, 2, :], c.MRT[:, 5, :], AF.Exp, bias=0.0, scale=-1.0)
        c.misc["rt2c"] = c.sACT.tok(s.drain())

    @blk.vector
    def _(v):
        waits(v, [c.misc["rt2c"]])
        for g8 in range(8):
            v.tensor_scalar(c.W64[:, g8 * 8:(g8 + 1) * 8], c.LG[:, g8 * 8:(g8 + 1) * 8],
                            c.MRT[:, 2, g8:g8 + 1], None, ALU.mult)
        w3 = c.W64.rearrange("p (t e) -> p t e", e=8)
        i = v.tensor_reduce(c.MRT[:, 3, :], w3, AX.X, ALU.max)
        waits(v, [c.sDVE.tok(i)])
        i = None
        for g8 in range(8):
            i = v.tensor_scalar(c.LG[:, g8 * 8:(g8 + 1) * 8], c.W64[:, g8 * 8:(g8 + 1) * 8],
                            c.MRT[:, 3, g8:g8 + 1], None, ALU.subtract)
        c.misc["rt3"] = c.sDVE.tok(i)

    @blk.scalar
    def _(s):
        waits(s, [c.misc["rt3"]])
        s.sign(c.LG[:], c.LG[:], bias=0.0)
        i = s.activation(c.LG[:], c.LG[:], AF.Relu, bias=0.0, scale=-1.0)
        c.misc["rt4"] = c.sACT.tok(s.drain())

    @blk.vector
    def _(v):
        waits(v, [c.misc["rt4"]])
        v.tensor_tensor(D64, c.W64[:], c.LG[:], ALU.mult)
        d3 = c.DELTA[:, 1, 0:64].rearrange("p (t e) -> p t e", e=8)
        i = v.tensor_reduce(c.MRT[:, 4, :], d3, AX.X, ALU.max)
        waits(v, [c.sDVE.tok(i)])
        for g8 in range(8):
            v.tensor_scalar(c.LG[:, g8 * 8:(g8 + 1) * 8], c.W64[:, g8 * 8:(g8 + 1) * 8],
                            c.MRT[:, 4, g8:g8 + 1], None, ALU.is_ge)
        v.tensor_tensor(c.W64[:], c.W64[:], c.LG[:], ALU.mult)
        i = v.tensor_scalar(c.W64[:], c.W64[:], 0.5, None, ALU.mult)
        tk = c.sDVE.tok(i)
        c.misc["w64"] = tk
        c.dl_trk[1].wdone(tk)

    def shared_pass(fs_list, koff, kcnt):
        for idx, fs in enumerate(fs_list):
            slots = []

            @blk.gpsimd
            def _(g, fs=fs, slots=slots):
                src = din["wsgt"][:, fs * 128:(fs + 1) * 128]
                slots.append(ws_issue(c, g, src.rearrange("(k p) m -> p k m", p=128),
                                      16 * 128))
                src = din["wsut"][:, fs * 128:(fs + 1) * 128]
                slots.append(ws_issue(c, g, src.rearrange("(k p) m -> p k m", p=128),
                                      16 * 128))

            @blk.tensor
            def _(t, fs=fs, slots=slots):
                tA.wr(t)
                tB.wr(t)
                c.ws_trk[slots[0]].rd(t)
                c.ws_trk[slots[1]].rd(t)
                waits(t, [c.xn_trk[k].w for k in range(DC)])
                lg_ = ws_view(c, slots[0], 16, 128)
                lu_ = ws_view(c, slots[1], 16, 128)
                lastg = lastu = None
                for k in range(DC):
                    for h0 in range(2):
                        sl = slice(h0 * 512, (h0 + 1) * 512)
                        lastg = t.matmul(PA[:, sl], lg_[:, k, :], c.XN[:, k, sl],
                                         start=(k == 0), stop=(k == DC - 1))
                    for h0 in range(2):
                        sl = slice(h0 * 512, (h0 + 1) * 512)
                        lastu = t.matmul(PB[:, sl], lu_[:, k, :], c.XN[:, k, sl],
                                         start=(k == 0), stop=(k == DC - 1))
                tkg = c.sPE.tok(lastg)
                tku = c.sPE.tok(lastu)
                tA.wdone(tkg)
                tB.wdone(tku)
                c.ws_trk[slots[0]].rdone(tkg)
                c.ws_trk[slots[1]].rdone(tku)

            @blk.scalar
            def _(s, fs=fs):
                tA.rd(s)
                if not hasattr(c, "sgb_trk"):
                    c.sgb_trk = Trk()
                c.sgb_trk.wr(s)
                i = s.activation(c.SGB[:], PA[:], AF.Silu, bias=0.0, scale=1.0)
                tk = c.sACT.tok(s.drain())
                tA.rdone(tk)
                c.sgb_trk.wdone(tk)

            @blk.vector
            def _(v, fs=fs, idx=idx):
                tB.rd(v)
                c.sgb_trk.rd(v)
                dst = idx if len(fs_list) == 6 else fs
                c.biga_trk[dst].wr(v)
                i = v.tensor_tensor(c.BIGA[:, dst, :], c.SGB[:], PB[:], ALU.mult)
                tk = c.sDVE.tok(i)
                tB.rdone(tk)
                c.sgb_trk.rdone(tk)
                c.biga_trk[dst].wdone(tk)

        for m in range(DC):
            slots = []
            psd, trkd = (PC, tC) if m % 2 == 0 else (PD, tD)

            @blk.gpsimd
            def _(g, m=m, slots=slots):
                src = din["wsdt"][koff:koff + kcnt * 128, m * 128:(m + 1) * 128]
                slots.append(ws_issue(c, g, src.rearrange("(k p) m -> p k m", p=128),
                                      kcnt * 128))

            @blk.tensor
            def _(t, m=m, slots=slots, psd=psd, trkd=trkd):
                trkd.wr(t)
                c.ws_trk[slots[0]].rd(t)
                for j in range(kcnt):
                    c.biga_trk[j].rd(t)
                lw = ws_view(c, slots[0], kcnt, 128)
                last = None
                for j in range(kcnt):
                    for h0 in range(2):
                        sl = slice(h0 * 512, (h0 + 1) * 512)
                        last = t.matmul(psd[:, sl], lw[:, j, :], c.BIGA[:, j, sl],
                                        start=(j == 0), stop=(j == kcnt - 1))
                tk = c.sPE.tok(last)
                trkd.wdone(tk)
                c.ws_trk[slots[0]].rdone(tk)
                if m == DC - 1:
                    for j in range(kcnt):
                        c.biga_trk[j].rdone(tk)

            @blk.vector
            def _(v, m=m, psd=psd, trkd=trkd):
                trkd.rd(v)
                i = v.scalar_tensor_tensor(c.R[:, m, :], psd[:], 0.5, c.R[:, m, :],
                                           ALU.mult, ALU.add)
                tk = c.sDVE.tok(i)
                trkd.rdone(tk)
                c.r_tok = tk

    if DEBUG_ATTN:
        @blk.sync
        def _(sy):
            waits(sy, [c.misc["w64"]])
            i = sy.dma_start(out=c.dbg_w64, in_=c.W64[:])
            c.misc["dbgw"] = c.sIO.tok(i, 16)

    shared_pass(list(range(16)), 0, 16)
    shared_pass(list(range(16, 22)), 16 * 128, 6)

    if DEBUG_ATTN:
        @blk.sync
        def _(sy):
            waits(sy, [c.r_tok])
            i = sy.dma_start(out=c.dbg_rsh.rearrange("(k p) m -> p k m", p=128), in_=c.R[:])
            c.misc["dbgrs"] = c.sIO.tok(i, 16)

    for e2 in range(E):
        @blk.vector
        def _(v, e2=e2):
            waits(v, [c.misc["w64"]])
            c.dl_trk[1].wr(v)
            i = None
            for tt in range(NT):
                i = v.tensor_scalar(c.DELTA[:, 1, tt * 128:(tt + 1) * 128],
                                    c.DELTA[:, 1, tt * 128:(tt + 1) * 128],
                                    0.0, c.W64[:, tt * 8 + e2:tt * 8 + e2 + 1],
                                    ALU.mult, op1=ALU.add)
            tk = c.sDVE.tok(i)
            c.misc["wb_src"] = tk
            c.dl_trk[1].wdone(tk)

        @blk.tensor
        def _(t, e2=e2):
            tC.wr(t)
            waits(t, [c.misc["wb_src"], c.misc["idf"]])
            last = None
            for tt in range(NT):
                last = t.transpose(PC[:, tt * 128:(tt + 1) * 128],
                                   c.DELTA[:, 1, tt * 128:(tt + 1) * 128], c.IDF[:])
            tk = c.sPE.tok(last)
            tC.wdone(tk)
            c.dl_trk[1].rdone(tk)

        for fc in range(FEC):
            slots = []

            @blk.gpsimd
            def _(g, e2=e2, fc=fc, slots=slots):
                src = din["wgt"][e2][:, fc * 128:(fc + 1) * 128]
                slots.append(ws_issue(c, g, src.rearrange("(k p) m -> p k m", p=128),
                                      16 * 128))
                src = din["wut"][e2][:, fc * 128:(fc + 1) * 128]
                slots.append(ws_issue(c, g, src.rearrange("(k p) m -> p k m", p=128),
                                      16 * 128))

            @blk.tensor
            def _(t, fc=fc, slots=slots):
                tA.wr(t)
                tB.wr(t)
                c.ws_trk[slots[0]].rd(t)
                c.ws_trk[slots[1]].rd(t)
                waits(t, [c.xn_trk[k].w for k in range(DC)])
                lg_ = ws_view(c, slots[0], 16, 128)
                lu_ = ws_view(c, slots[1], 16, 128)
                lastg = lastu = None
                for k in range(DC):
                    for h0 in range(2):
                        sl = slice(h0 * 512, (h0 + 1) * 512)
                        lastg = t.matmul(PA[:, sl], lg_[:, k, :], c.XN[:, k, sl],
                                         start=(k == 0), stop=(k == DC - 1))
                    for h0 in range(2):
                        sl = slice(h0 * 512, (h0 + 1) * 512)
                        lastu = t.matmul(PB[:, sl], lu_[:, k, :], c.XN[:, k, sl],
                                         start=(k == 0), stop=(k == DC - 1))
                tkg = c.sPE.tok(lastg)
                tku = c.sPE.tok(lastu)
                tA.wdone(tkg)
                tB.wdone(tku)
                c.ws_trk[slots[0]].rdone(tkg)
                c.ws_trk[slots[1]].rdone(tku)
                if fc == FEC - 1:
                    pass

            @blk.scalar
            def _(s, fc=fc):
                tA.rd(s)
                c.sgb_trk.wr(s)
                i = s.activation(c.SGB[:], PA[:], AF.Silu, bias=0.0, scale=1.0)
                tk = c.sACT.tok(s.drain())
                tA.rdone(tk)
                c.sgb_trk.wdone(tk)

            @blk.vector
            def _(v, fc=fc):
                tB.rd(v)
                c.sgb_trk.rd(v)
                if not hasattr(c, "hb_trk"):
                    c.hb_trk = Trk()
                c.hb_trk.wr(v)
                v.tensor_tensor(c.HB[:], c.SGB[:], PB[:], ALU.mult)
                tkp = (c.sDVE, c.sDVE.n + 1)
                tC.rd(v)
                c.biga_trk[fc].wr(v)
                i = v.tensor_tensor(c.BIGA[:, fc, :], c.HB[:], PC[:], ALU.mult)
                tk = c.sDVE.tok(i)
                tB.rdone(tk)
                c.sgb_trk.rdone(tk)
                c.hb_trk.wdone(tk)
                tC.rdone(tk)
                c.biga_trk[fc].wdone(tk)

        for m in range(DC):
            slots = []
            psd, trkd = (PD, tD) if m % 2 == 0 else (PB, tB)

            @blk.gpsimd
            def _(g, e2=e2, m=m, slots=slots):
                src = din["wdt"][e2][:, m * 128:(m + 1) * 128]
                slots.append(ws_issue(c, g, src.rearrange("(k p) m -> p k m", p=128),
                                      FEC * 128))

            @blk.tensor
            def _(t, m=m, slots=slots, psd=psd, trkd=trkd):
                trkd.wr(t)
                c.ws_trk[slots[0]].rd(t)
                for j in range(FEC):
                    c.biga_trk[j].rd(t)
                lw = ws_view(c, slots[0], FEC, 128)
                last = None
                for j in range(FEC):
                    for h0 in range(2):
                        sl = slice(h0 * 512, (h0 + 1) * 512)
                        last = t.matmul(psd[:, sl], lw[:, j, :], c.BIGA[:, j, sl],
                                        start=(j == 0), stop=(j == FEC - 1))
                tk = c.sPE.tok(last)
                trkd.wdone(tk)
                c.ws_trk[slots[0]].rdone(tk)
                if m == DC - 1:
                    for j in range(FEC):
                        c.biga_trk[j].rdone(tk)

            @blk.vector
            def _(v, m=m, e2=e2, psd=psd, trkd=trkd):
                trkd.rd(v)
                i = v.tensor_tensor(c.R[:, m, :], c.R[:, m, :], psd[:], ALU.add)
                tk = c.sDVE.tok(i)
                trkd.rdone(tk)
                c.r_tok = tk
                if m == DC - 1 and e2 == E - 1:
                    for k in range(DC):
                        c.xn_trk[k].rdone(tk)



_PHASES = ["pre0", "attn0", "post0", "moe", "attn1", "moe"]


def kernel(**inputs):
    maps = host_prep(inputs)
    nc, needed = build_program(_PHASES, first=True)
    in_maps = [{k: m[k] for k in needed} for m in maps]
    res = run_bass_kernel_spmd(nc, in_maps, core_ids=list(range(8)))
    return host_assemble(res.results)


# version 07
# speedup vs baseline: 1.0196x; 1.0196x over previous
import numpy as np
import ml_dtypes
import concourse.bass as bass
import concourse.mybir as mybir
from concourse.bass_utils import run_bass_kernel_spmd
from contextlib import ExitStack

F32 = mybir.dt.float32
BF16 = mybir.dt.bfloat16
AF = mybir.ActivationFunctionType
ALU = mybir.AluOpType
AX = mybir.AxisListType

D = 2048
DC = 16
T = 1024
NT = 8
H = 16
HD = 128
FFC = 64
FSC = 22
FEC = 11
E = 8
QSCALE = 1.0 / np.sqrt(128.0)
C1 = 0.5 / np.sqrt(8.0)

BF = ml_dtypes.bfloat16
DEBUG_ATTN = False
DBG_L = 0
AG_OVERLAP = False



def core_tiles(par):
    return list(range(1, 16, 2)) if par == 0 else list(range(0, 16, 2))


def host_prep(inp):
    x = np.asarray(inp["L0_attn_x"])
    slopes = 2.0 ** (-(np.arange(1, 17)) / 2.0)
    tri = np.triu(np.ones((128, 128), np.float32))
    onesm = np.ones((128, 128), np.float32)
    zerosm = np.zeros((128, 128), np.float32)

    def br(b, n):
        return np.ascontiguousarray(np.asarray(b).reshape(n, 128).T)

    shared = {
        "wqkv": np.asarray(inp["L0_attn_w_qkv"]),
        "wproj": np.asarray(inp["L0_attn_w_proj"]),
        "wff1": np.asarray(inp["L0_attn_w_ff1"]),
        "bff1": br(inp["L0_attn_b_ff1"], 64),
        "wff2": np.asarray(inp["L0_attn_w_ff2"]),
        "bff2": br(inp["L0_attn_b_ff2"], 16),
        "bffn0": br(inp["L0_ffn_bias"], 16),
        "wq": np.asarray(inp["L1_attn_Wq"]),
        "wk": np.asarray(inp["L1_attn_Wk"]),
        "wv": np.asarray(inp["L1_attn_Wv"]),
        "wo": np.asarray(inp["L1_attn_Wo"]),
        "bq": br(inp["L1_attn_bq"], 16),
        "bk": br(inp["L1_attn_bk"], 16),
        "bo": br(inp["L1_attn_bo"], 16),
        "bvrow": np.asarray(inp["L1_attn_bv"]).reshape(1, 2048),
        "wrt": np.ascontiguousarray(np.asarray(inp["L1_ffn_Wr"]).T),
        "wsgt": np.ascontiguousarray(np.asarray(inp["L1_ffn_Wsg"]).T),
        "wsut": np.ascontiguousarray(np.asarray(inp["L1_ffn_Wsu"]).T),
        "wsdt": np.ascontiguousarray(np.asarray(inp["L1_ffn_Wsd"]).T),
        "wgt": np.ascontiguousarray(np.asarray(inp["L1_ffn_Wg"]).transpose(0, 2, 1)),
        "wut": np.ascontiguousarray(np.asarray(inp["L1_ffn_Wu"]).transpose(0, 2, 1)),
        "wdt": np.ascontiguousarray(np.asarray(inp["L1_ffn_Wd"]).transpose(0, 2, 1)),
    }
    acol = np.zeros((128, 256), np.float32)
    for h in range(16):
        for kc in range(16):
            acol[:, h * 16 + kc] = slopes[h] * ((kc * 128 + np.arange(128)) - 2047)
    shared["acol"] = acol

    factors = np.asarray(inp["L0_ffn_factors"])
    maps = []
    for c in range(8):
        s, par = c // 2, c % 2
        tiles = core_tiles(par)
        toks = np.concatenate([np.arange(t * 128, (t + 1) * 128) for t in tiles])
        m = dict(shared)
        m["xs"] = np.ascontiguousarray(x[s][toks].T)
        m["fac"] = np.ascontiguousarray(factors[:, 256 * c:256 * (c + 1), :])
        if par == 0:
            m["maskA"], m["maskB"] = onesm, tri
            m["amaskA"], m["amaskB"] = zerosm, (1.0 - tri) * -1e30
        else:
            m["maskA"], m["maskB"] = tri, zerosm
            m["amaskA"], m["amaskB"] = (1.0 - tri) * -1e30, np.full((128, 128), -1e30, np.float32)
        crow = np.zeros((16, 1024), np.float32)
        for h in range(16):
            for j, t in enumerate(tiles):
                crow[h, j * 128:(j + 1) * 128] = slopes[h] * (t * 128 + np.arange(128) - 2047)
        m["crow"] = crow.reshape(1, 16 * 1024)
        maps.append(m)
    return maps


def host_assemble(results):
    out = np.empty((4, 2048, 2048), np.float32)
    for c in range(8):
        s, par = c // 2, c % 2
        tiles = core_tiles(par)
        toks = np.concatenate([np.arange(t * 128, (t + 1) * 128) for t in tiles])
        out[s][toks] = results[c]["r_out"].T
    return out



class Sem:
    __slots__ = ("h", "n")

    def __init__(self, h):
        self.h = h
        self.n = 0

    def tok(self, instr, k=1):
        instr.then_inc(self.h, k)
        self.n += k
        return (self, self.n)


def waits(eng, toks):
    best = {}
    for t in toks:
        if t is None:
            continue
        s, v = t
        if v > best.get(id(s), (None, 0))[1]:
            best[id(s)] = (s, v)
    for s, v in best.values():
        eng.wait_ge(s.h, v)


class Trk:
    __slots__ = ("w", "r")

    def __init__(self):
        self.w = None
        self.r = []

    def rd(self, eng):
        waits(eng, [self.w])

    def rdone(self, tok):
        self.r.append(tok)

    def wr(self, eng):
        waits(eng, self.r + [self.w])

    def wdone(self, tok):
        self.w = tok
        self.r = []


class Ctx:
    pass


PH_INPUTS = {
    "pre0": ["fac", "bffn0"],
    "attn0": ["wqkv", "wproj", "wff1", "bff1", "wff2", "bff2", "maskA", "maskB"],
    "post0": ["fac", "bffn0"],
    "moe": ["wrt", "wsgt", "wsut", "wsdt", "wgt", "wut", "wdt"],
    "attn1": ["wq", "wk", "wv", "wo", "bq", "bk", "bo", "bvrow", "amaskA", "amaskB",
              "acol", "crow"],
}

IN_SHAPES = {
    "xs": ([2048, 1024], F32), "fac": ([8, 256, 2048], F32),
    "wqkv": ([2048, 6144], F32), "wproj": ([2048, 2048], F32),
    "wff1": ([2048, 8192], F32), "bff1": ([128, 64], F32),
    "wff2": ([8192, 2048], F32), "bff2": ([128, 16], F32),
    "bffn0": ([128, 16], F32),
    "wq": ([2048, 2048], F32), "wk": ([2048, 2048], F32), "wv": ([2048, 2048], F32),
    "wo": ([2048, 2048], F32), "bq": ([128, 16], F32), "bk": ([128, 16], F32),
    "bo": ([128, 16], F32), "bvrow": ([1, 2048], F32),
    "wrt": ([2048, 8], F32), "wsgt": ([2048, 2816], F32), "wsut": ([2048, 2816], F32),
    "wsdt": ([2816, 2048], F32), "wgt": ([8, 2048, 1408], F32),
    "wut": ([8, 2048, 1408], F32), "wdt": ([8, 1408, 2048], F32),
    "maskA": ([128, 128], F32), "maskB": ([128, 128], F32),
    "acol": ([128, 256], F32), "crow": ([1, 16384], F32),
    "amaskA": ([128, 128], F32), "amaskB": ([128, 128], F32),
}


def build_program(phases, first=False, last=False):
    nc = bass.Bass("TRN2", target_bir_lowering=False)
    c = Ctx()
    c.nc = nc
    c.phases = phases

    need = set()
    for p in phases:
        need.update(PH_INPUTS[p])
    if first:
        need.add("xs")
    c.inputs_needed = sorted(need) + ([] if first else ["r_in"])

    dram_in = {}
    for nm in sorted(need):
        sh, dt = IN_SHAPES[nm]
        dram_in[nm] = nc.dram_tensor(nm, sh, dt, kind="ExternalInput").ap()
    if not first:
        dram_in["r_in"] = nc.dram_tensor("r_in", [2048, 1024], F32, kind="ExternalInput").ap()
    c.din = dram_in
    c.r_out = nc.dram_tensor("r_out", [2048, 1024], F32, kind="ExternalOutput").ap()
    if DEBUG_ATTN:
        c.dbg_q = nc.dram_tensor("dbg_q", [2048, 1024], F32, kind="ExternalOutput").ap()
        c.dbg_ctx = nc.dram_tensor("dbg_ctx", [2048, 1024], F32, kind="ExternalOutput").ap()
        c.dbg_kvg = nc.dram_tensor("dbg_kvg", [8192, 1024], F32, kind="ExternalOutput").ap()
        c.dbg_kvc = nc.dram_tensor("dbg_kvc", [4096, 1024], F32, kind="ExternalOutput").ap()
        c.dbg_h0 = nc.dram_tensor("dbg_h0", [1024, 1024], F32, kind="ExternalOutput").ap()
        c.dbg_w64 = nc.dram_tensor("dbg_w64", [128, 64], F32, kind="ExternalOutput").ap()
        c.dbg_rsh = nc.dram_tensor("dbg_rsh", [2048, 1024], F32, kind="ExternalOutput").ap()
        c.dbg_rff = nc.dram_tensor("dbg_rff", [2048, 1024], F32, kind="ExternalOutput").ap()

    if "pre0" in phases or "post0" in phases:
        c.fslice = nc.dram_tensor("fslice", [256, 2048], BF16)
        c.fgath = nc.dram_tensor("fgath", [2048, 2048], BF16)
    for L, ph in ((0, "attn0"), (1, "attn1")):
        if ph in phases:
            setattr(c, f"kvcon{L}", nc.dram_tensor(f"kvcon{L}", [4096, 1024], BF16))
            setattr(c, f"kvg{L}", nc.dram_tensor(f"kvg{L}", [8192, 1024], BF16))

    stk = ExitStack()
    c.stk = stk
    e = stk.enter_context
    c.R = e(nc.sbuf_tensor("R", [128, 16, 1024], F32))
    c.XN = e(nc.sbuf_tensor("XN", [128, 16, 1024], BF16))
    c.BIGA = e(nc.sbuf_tensor("BIGA", [128, 16, 1024], BF16))
    c.WS = e(nc.sbuf_tensor("WS", [128, 4, 2048], BF16))
    c.SQ = e(nc.sbuf_tensor("SQ", [128, 2, 1024], BF16))
    c.DELTA = e(nc.sbuf_tensor("DELTA", [128, 2, 1024], F32))
    c.ROWS = e(nc.sbuf_tensor("ROWS", [1, 3072], F32))
    c.ONESB = e(nc.sbuf_tensor("ONESB", [1, 128], BF16))
    c.ONESF = e(nc.sbuf_tensor("ONESF", [1, 128], F32))
    c.ONESC = e(nc.sbuf_tensor("ONESC", [128, 1], BF16))
    c.BIAS = e(nc.sbuf_tensor("BIAS", [128, 176], F32))
    c.KH = e(nc.sbuf_tensor("KH", [128, 2, 2048], BF16))
    c.VTH = e(nc.sbuf_tensor("VTH", [128, 2, 16, 128], BF16))
    c.PT = e(nc.sbuf_tensor("PT", [128, 2, 1024], BF16))
    c.CRB = e(nc.sbuf_tensor("CRB", [128, 1024], F32))
    c.MASK = e(nc.sbuf_tensor("MASK", [128, 256], BF16))
    c.AMASK = e(nc.sbuf_tensor("AMASK", [128, 256], F32))
    c.ACOL = e(nc.sbuf_tensor("ACOL", [128, 256], F32))
    c.BVR = e(nc.sbuf_tensor("BVR", [1, 1024], BF16))
    c.STG = e(nc.sbuf_tensor("STG", [128, 2, 1024], BF16))
    c.LG = e(nc.sbuf_tensor("LG", [128, 64], F32))
    c.MRT = e(nc.sbuf_tensor("MRT", [128, 7, 8], F32))
    c.W64 = e(nc.sbuf_tensor("W64", [128, 64], F32))
    c.WRTS = e(nc.sbuf_tensor("WRTS", [128, 128], BF16))
    c.SGB = e(nc.sbuf_tensor("SGB", [128, 1024], BF16))
    c.HB = e(nc.sbuf_tensor("HB", [128, 1024], BF16))
    c.IDF = e(nc.sbuf_tensor("IDF", [128, 128], F32))

    c.PS = [e(nc.psum_tensor(f"P{i}", [128, 1024], F32)) for i in range(4)]
    c.pstrk = [Trk() for _ in range(4)]

    for nm in ["sWL", "sIO", "sPE", "sACT", "sDVE", "sGP", "sCC", "sCC2",
               "sCC3", "sCC4", "sCC5", "sOUT"]:
        setattr(c, nm, Sem(e(nc.semaphore(nm))))

    c.xn_trk = [Trk() for _ in range(DC)]
    c.sq_trk = [Trk() for _ in range(2)]
    c.ws_trk = [Trk() for _ in range(4)]
    c.dl_trk = [Trk() for _ in range(2)]
    c.ws_next = 0
    c.r_tok = None
    c.misc = {}

    c.block = e(nc.Block())

    emit_prologue(c, first)
    for p in phases:
        if p == "pre0" or p == "post0":
            emit_ffn0(c)
        elif p == "attn0":
            emit_attn(c, 0)
        elif p == "attn1":
            emit_attn(c, 1)
        elif p == "moe":
            emit_moe(c)
    emit_epilogue(c)
    stk.close()
    return nc, c.inputs_needed



def emit_prologue(c, first):
    blk = c.block
    din = c.din
    need_f = "pre0" in c.phases or "post0" in c.phases

    if need_f:
        @blk.gpsimd
        def _(g):
            rview = c.R[:].rearrange("p k m -> p (k m)")[:, 0:4096] \
                        .rearrange("p (q m) -> p q m", m=2048)
            last = None
            for r in range(8):
                if last is not None:
                    waits(g, [last])
                src = c.din["fac"][r].rearrange("(q p) m -> p q m", p=128)
                if r == 0:
                    i = g.dma_start(out=rview, in_=src)
                else:
                    i = g.dma_start(out=rview, in_=src, accum_op=ALU.add)
                last = c.sWL.tok(i, 16)
            waits(g, [last])
            i = g.dma_start(out=c.fslice.ap().rearrange("(q p) m -> p q m", p=128),
                            in_=rview)
            fst = c.sWL.tok(i, 16)
            c.misc["fslice_done"] = fst
            waits(g, [fst])
            i = g.collective_compute(
                "AllGather", ALU.bypass, replica_groups=[list(range(8))],
                ins=[c.fslice.ap().opt()], outs=[c.fgath.ap().opt()])
            c.misc["f_ready"] = c.sCC.tok(i)

    @blk.sync
    def _(sy):
        if need_f:
            waits(sy, [c.misc["fslice_done"]])
        src = din["xs"] if first else din["r_in"]
        i = sy.dma_start(out=c.R[:], in_=src.rearrange("(k p) m -> p k m", p=128))
        c.misc["rload"] = c.sIO.tok(i, 16)

    @blk.gpsimd
    def _(g):
        g.memset(c.ONESB[:], 1.0)
        g.memset(c.ONESF[:], 1.0)
        i = g.memset(c.ONESC[:], 1.0)
        c.misc["ones"] = c.sGP.tok(i)
        g.memset(c.IDF[:], 1.0)
        i = g.affine_select(c.IDF[:], c.IDF[:], [[1, 128]], ALU.is_equal, 0.0,
                            base=0, channel_multiplier=-1)
        c.misc["idf"] = c.sGP.tok(i)
        toks = []
        for nm, off in BOFF.items():
            if nm in din:
                n = IN_SHAPES[nm][0][1]
                i = g.dma_start(out=c.BIAS[:, off:off + n], in_=din[nm][:, :])
                toks.append(c.sWL.tok(i, 16))
        if "maskA" in din:
            i = g.dma_start(out=c.MASK[:, 0:128], in_=din["maskA"][:])
            toks.append(c.sWL.tok(i, 16))
            i = g.dma_start(out=c.MASK[:, 128:256], in_=din["maskB"][:])
            toks.append(c.sWL.tok(i, 16))
        if "acol" in din:
            i = g.dma_start(out=c.ACOL[:], in_=din["acol"][:])
            toks.append(c.sWL.tok(i, 16))
        if "amaskA" in din:
            i = g.dma_start(out=c.AMASK[:, 0:128], in_=din["amaskA"][:])
            toks.append(c.sWL.tok(i, 16))
            i = g.dma_start(out=c.AMASK[:, 128:256], in_=din["amaskB"][:])
            toks.append(c.sWL.tok(i, 16))
        c.misc["bias_toks"] = toks

    @blk.vector
    def _(v):
        waits(v, c.misc["bias_toks"])
        t = None
        if "bffn0" in din:
            i = v.tensor_scalar(c.BIAS[:, DOFF["hbffn0"]:DOFF["hbffn0"] + 16],
                                c.BIAS[:, BOFF["bffn0"]:BOFF["bffn0"] + 16],
                                0.5, None, ALU.mult)
            t = c.sDVE.tok(i)
        if "bq" in din:
            i = v.tensor_scalar(c.BIAS[:, DOFF["sbq"]:DOFF["sbq"] + 16],
                                c.BIAS[:, BOFF["bq"]:BOFF["bq"] + 16],
                                QSCALE, None, ALU.mult)
            t = c.sDVE.tok(i)
        c.misc["dbias"] = t



BOFF = {"bff1": 0, "bff2": 64, "bffn0": 80, "bq": 96, "bk": 112, "bo": 128}
DOFF = {"hbffn0": 144, "sbq": 160}



def emit_epilogue(c):
    blk = c.block

    @blk.sync
    def _(sy):
        waits(sy, [c.r_tok, c.misc.get("dbg0"), c.misc.get("dbg1"), c.misc.get("dbgh"), c.misc.get("dbgr"), c.misc.get("dbgw"), c.misc.get("dbgrs"), c.misc.get("dbg2"), c.misc.get("dbg3")])
        i = sy.dma_start(out=c.r_out.rearrange("(k p) m -> p k m", p=128), in_=c.R[:])
        t = c.sOUT.tok(i, 16)
        waits(sy, [t])



def emit_ln(c, add_n1=False):
    blk = c.block
    PA, PB, PC, PD = c.PS
    tA, tB, tC, tD = c.pstrk

    cast_toks = []

    @blk.vector
    def _(v):
        waits(v, [c.misc.get("rload"), c.r_tok])
        for d in range(DC):
            c.xn_trk[d].wr(v)
            i = v.tensor_copy(c.XN[:, d, :], c.R[:, d, :])
            tk = c.sDVE.tok(i)
            cast_toks.append(tk)
            c.xn_trk[d].wdone(tk)

    sq_toks = []

    @blk.scalar
    def _(s):
        for d in range(DC):
            waits(s, [cast_toks[d]])
            c.sq_trk[d % 2].wr(s)
            i = s.activation(c.SQ[:, d % 2, :], c.XN[:, d, :], AF.Square)
            sq_toks.append(c.sACT.tok(s.drain()))

    @blk.tensor
    def _(t):
        tC.wr(t)
        tD.wr(t)
        waits(t, [c.misc["ones"]])
        mm_sum = mm_sq = None
        tok_sum = tok_sq = None
        for d in range(DC):
            waits(t, [cast_toks[d]])
            for h0 in range(2):
                sl = slice(h0 * 512, (h0 + 1) * 512)
                mm_sum = t.matmul(PC[0:1, sl], c.ONESC[:], c.XN[:, d, sl],
                                  start=(d == 0), stop=(d == DC - 1))
            if d == DC - 1:
                tok_sum = c.sPE.tok(mm_sum)
            waits(t, [sq_toks[d]])
            for h0 in range(2):
                sl = slice(h0 * 512, (h0 + 1) * 512)
                mm_sq = t.matmul(PD[0:1, sl], c.ONESC[:], c.SQ[:, d % 2, sl],
                                 start=(d == 0), stop=(d == DC - 1))
            tok_sq = c.sPE.tok(mm_sq)
            c.sq_trk[d % 2].rdone(tok_sq)
        tC.wdone(tok_sum)
        tD.wdone(tok_sq)

    MEAN, RSTD, SCR = slice(0, 1024), slice(1024, 2048), slice(2048, 3072)
    MR = MEAN
    row_tok = []

    @blk.vector
    def _(v):
        tC.rd(v)
        tD.rd(v)
        i = v.tensor_scalar(c.ROWS[0:1, MEAN], PC[0:1, :], 1.0 / 2048.0, None, ALU.mult)
        tC.rdone(c.sDVE.tok(i))
        i = v.tensor_scalar(c.ROWS[0:1, SCR], PD[0:1, :], 1.0 / 2048.0, 1e-5,
                            ALU.mult, op1=ALU.add)
        tD.rdone(c.sDVE.tok(i))
        i = v.tensor_tensor(c.ROWS[0:1, RSTD], c.ROWS[0:1, MEAN], c.ROWS[0:1, MEAN],
                            ALU.mult)
        i = v.tensor_tensor(c.ROWS[0:1, SCR], c.ROWS[0:1, SCR], c.ROWS[0:1, RSTD],
                            ALU.subtract)
        row_tok.append(c.sDVE.tok(i))

    @blk.scalar
    def _(s):
        waits(s, row_tok)
        i = s.activation(c.ROWS[0:1, SCR], c.ROWS[0:1, SCR], AF.Ln, bias=0.0, scale=1.0)
        i = s.activation(c.ROWS[0:1, RSTD], c.ROWS[0:1, SCR], AF.Exp, bias=0.0, scale=-0.5)
        row_tok.append(c.sACT.tok(s.drain()))

    @blk.vector
    def _(v):
        waits(v, row_tok)
        i = v.tensor_tensor(c.ROWS[0:1, MR], c.ROWS[0:1, MEAN], c.ROWS[0:1, RSTD],
                            ALU.mult)
        row_tok.append(c.sDVE.tok(i))

    @blk.tensor
    def _(t):
        tA.wr(t)
        tB.wr(t)
        waits(t, row_tok)
        for h0 in range(2):
            sl = slice(h0 * 512, (h0 + 1) * 512)
            ia = t.matmul(PA[:, sl], c.ONESF[:], c.ROWS[0:1, 1024 + h0 * 512:1024 + (h0 + 1) * 512],
                          start=True, stop=True)
            ib = t.matmul(PB[:, sl], c.ONESF[:], c.ROWS[0:1, h0 * 512:(h0 + 1) * 512],
                          start=True, stop=True)
        tA.wdone(c.sPE.tok(ia))
        tB.wdone(c.sPE.tok(ib))

    @blk.vector
    def _(v):
        tA.rd(v)
        tB.rd(v)
        c.dl_trk[1].wr(v)
        if add_n1:
            c.dl_trk[0].wr(v)
        tmp = c.DELTA[:, 1, :]
        i = None
        for d in range(DC):
            v.tensor_tensor(tmp, c.R[:, d, :], PA[:], ALU.mult)
            if not add_n1:
                i = v.tensor_tensor(c.XN[:, d, :], tmp, PB[:], ALU.subtract)
            else:
                t2 = c.DELTA[:, 0, :]
                v.tensor_tensor(t2, tmp, PB[:], ALU.subtract)
                v.tensor_tensor(c.R[:, d, :], c.R[:, d, :], t2, ALU.add)
                i = v.tensor_copy(c.XN[:, d, :], t2)
            tk = c.sDVE.tok(i)
            c.xn_trk[d].wdone(tk)
        tA.rdone((c.sDVE, c.sDVE.n))
        tB.rdone((c.sDVE, c.sDVE.n))
        c.dl_trk[1].wdone((c.sDVE, c.sDVE.n))
        if add_n1:
            c.r_tok = (c.sDVE, c.sDVE.n)
            c.dl_trk[0].wdone((c.sDVE, c.sDVE.n))



def ws_issue(c, g, src_ap, nelem):
    slot = c.ws_next
    c.ws_next = (c.ws_next + 1) % 4
    trk = c.ws_trk[slot]
    trk.wr(g)
    dst = c.WS[:, slot, 0:nelem]
    shp = src_ap.shape
    if len(shp) == 3:
        dst = dst.rearrange("p (k m) -> p k m", m=shp[2])
    i = g.dma_start(out=dst, in_=src_ap)
    tk = c.sWL.tok(i, 16)
    trk.wdone(tk)
    return slot


def ws_view(c, slot, k, m):
    return c.WS[:, slot, 0:k * m].rearrange("p (k m) -> p k m", m=m)



def emit_ffn0(c):
    emit_ln(c, add_n1=False)
    blk = c.block
    PA, PB = c.PS[0], c.PS[1]
    tA, tB = c.pstrk[0], c.pstrk[1]
    fg = c.fgath.ap()

    GM = 1
    for g0 in range(0, DC, GM):
        slotbox = []

        @blk.gpsimd
        def _(g, g0=g0):
            if g0 == 0:
                waits(g, [c.misc["f_ready"]])
            src = fg[:, g0 * 128:(g0 + GM) * 128].rearrange("(k p) m -> p k m", p=128)
            slotbox.append(ws_issue(c, g, src, 16 * GM * 128))

        for mi in range(GM):
            m = g0 + mi
            ps, trk = (PA, tA) if m % 2 == 0 else (PB, tB)

            @blk.tensor
            def _(t, m=m, mi=mi, ps=ps, trk=trk, slotbox=slotbox):
                slot = slotbox[0]
                trk.wr(t)
                c.ws_trk[slot].rd(t)
                waits(t, [c.xn_trk[k].w for k in range(DC)])
                lhsv = ws_view(c, slot, 16, GM * 128)
                last = None
                for k in range(DC):
                    lh = lhsv[:, k, mi * 128:(mi + 1) * 128]
                    for h0 in range(2):
                        sl = slice(h0 * 512, (h0 + 1) * 512)
                        last = t.matmul(ps[:, sl], lh, c.XN[:, k, sl],
                                        start=(k == 0), stop=(k == DC - 1))
                tk = c.sPE.tok(last)
                trk.wdone(tk)
                if mi == GM - 1:
                    c.ws_trk[slot].rdone(tk)
                if m == DC - 1:
                    for d in range(DC):
                        c.xn_trk[d].rdone(tk)

            @blk.scalar
            def _(s, m=m, ps=ps, trk=trk):
                trk.rd(s)
                c.dl_trk[m % 2].wr(s)
                hb = c.BIAS[:, DOFF["hbffn0"] + m:DOFF["hbffn0"] + m + 1]
                waits(s, [c.misc.get("dbias")])
                i = s.activation(c.DELTA[:, m % 2, :], ps[:], AF.Identity,
                                 bias=hb, scale=C1)
                tk = c.sACT.tok(s.drain())
                trk.rdone(tk)
                c.dl_trk[m % 2].wdone(tk)

            @blk.vector
            def _(v, m=m):
                c.dl_trk[m % 2].rd(v)
                i = v.tensor_tensor(c.R[:, m, :], c.R[:, m, :], c.DELTA[:, m % 2, :],
                                    ALU.add)
                c.r_tok = c.sDVE.tok(i)
                c.dl_trk[m % 2].rdone(c.r_tok)



def emit_attn(c, L):
    blk = c.block
    PA, PB, PC, PD = c.PS
    tA, tB, tC, tD = c.pstrk
    kvcon = getattr(c, f"kvcon{L}").ap()
    kvg = getattr(c, f"kvg{L}").ap()
    agsem = c.sCC2 if L == 0 else c.sCC4
    din = c.din

    if not hasattr(c, "biga_trk"):
        c.biga_trk = [Trk() for _ in range(16)]
        c.stg_trk = [Trk() for _ in range(2)]
        c.kh_trk = [Trk() for _ in range(2)]
        c.vth_trk = [Trk() for _ in range(2)]
        c.crb_trk = Trk()

    emit_ln(c, add_n1=(L == 0))

    kproj_toks = []
    for m in range(DC):
        slotbox = []

        @blk.gpsimd
        def _(g, m=m, slotbox=slotbox):
            if L == 0:
                src = din["wqkv"][:, 2048 + m * 128:2048 + (m + 1) * 128]
            else:
                src = din["wk"][:, m * 128:(m + 1) * 128]
            slotbox.append(ws_issue(c, g, src.rearrange("(k p) m -> p k m", p=128),
                                    16 * 128))

        ps, trk = (PA, tA) if m % 2 == 0 else (PB, tB)

        @blk.tensor
        def _(t, m=m, ps=ps, trk=trk, slotbox=slotbox):
            slot = slotbox[0]
            trk.wr(t)
            c.ws_trk[slot].rd(t)
            waits(t, [c.xn_trk[k].w for k in range(DC)])
            lhsv = ws_view(c, slot, 16, 128)
            last = None
            for k in range(DC):
                for h0 in range(2):
                    sl = slice(h0 * 512, (h0 + 1) * 512)
                    last = t.matmul(ps[:, sl], lhsv[:, k, :], c.XN[:, k, sl],
                                    start=(k == 0), stop=(k == DC - 1))
            tk = c.sPE.tok(last)
            trk.wdone(tk)
            c.ws_trk[slot].rdone(tk)

        @blk.scalar
        def _(s, m=m, ps=ps, trk=trk):
            trk.rd(s)
            c.stg_trk[m % 2].wr(s)
            if L == 0:
                i = s.activation(c.STG[:, m % 2, :], ps[:], AF.Identity,
                                 bias=0.0, scale=1.0)
            else:
                i = s.activation(c.STG[:, m % 2, :], ps[:], AF.Identity,
                                 bias=c.BIAS[:, BOFF["bk"] + m:BOFF["bk"] + m + 1],
                                 scale=1.0)
            tk = c.sACT.tok(s.drain())
            trk.rdone(tk)
            c.stg_trk[m % 2].wdone(tk)

        @blk.sync
        def _(sy, m=m):
            c.stg_trk[m % 2].rd(sy)
            i = sy.dma_start(out=kvcon[m * 128:(m + 1) * 128, :], in_=c.STG[:, m % 2, :])
            tk = c.sIO.tok(i, 16)
            c.stg_trk[m % 2].rdone(tk)
            kproj_toks.append(tk)

    vproj_toks = []
    if not hasattr(c, "bvr_trk"):
        c.bvr_trk = Trk()
    for half in range(2):
        @blk.gpsimd
        def _(g, half=half):
            if L == 1:
                c.bvr_trk.wr(g)
                i = g.dma_start(out=c.BVR[:],
                                in_=din["bvrow"][0:1, half * 1024:(half + 1) * 1024])
                c.bvr_trk.wdone(c.sWL.tok(i, 16))
            for d in range(16):
                c.biga_trk[d].wr(g)
            if L == 0:
                src = din["wqkv"][:, 4096 + half * 1024:4096 + (half + 1) * 1024]
            else:
                src = din["wv"][:, half * 1024:(half + 1) * 1024]
            i = g.dma_start(out=c.BIGA[:],
                            in_=src.rearrange("(k p) m -> p k m", p=128))
            tk = c.sWL.tok(i, 16)
            for d in range(16):
                c.biga_trk[d].wdone(tk)

        for tt in range(NT):
            ps, trk = (PA, tA) if tt % 2 == 0 else (PB, tB)

            @blk.tensor
            def _(t, half=half, tt=tt, ps=ps, trk=trk):
                trk.wr(t)
                for d in range(16):
                    c.biga_trk[d].rd(t)
                waits(t, [c.xn_trk[k].w for k in range(DC)] + [c.misc["ones"]])
                last = None
                for k in range(DC):
                    lh = c.XN[:, k, tt * 128:(tt + 1) * 128]
                    laststep = (k == DC - 1) and (L == 0)
                    for h0 in range(2):
                        sl = slice(h0 * 512, (h0 + 1) * 512)
                        last = t.matmul(ps[:, sl], lh, c.BIGA[:, k, sl],
                                        start=(k == 0), stop=laststep)
                if L == 1:
                    c.bvr_trk.rd(t)
                    for h0 in range(2):
                        sl = slice(h0 * 512, (h0 + 1) * 512)
                        last = t.matmul(ps[:, sl], c.ONESB[:], c.BVR[0:1, sl],
                                        start=False, stop=True)
                tk = c.sPE.tok(last)
                if L == 1 and tt == NT - 1:
                    c.bvr_trk.rdone(tk)
                trk.wdone(tk)
                for d in range(16):
                    c.biga_trk[d].rdone(tk)

            @blk.scalar
            def _(s, half=half, tt=tt, ps=ps, trk=trk):
                trk.rd(s)
                c.stg_trk[tt % 2].wr(s)
                i = s.activation(c.STG[:, tt % 2, :], ps[:], AF.Identity,
                                 bias=0.0, scale=1.0)
                tk = c.sACT.tok(s.drain())
                trk.rdone(tk)
                c.stg_trk[tt % 2].wdone(tk)

            @blk.sync
            def _(sy, half=half, tt=tt):
                c.stg_trk[tt % 2].rd(sy)
                vdst = kvcon[2048:4096, :].rearrange("(t two) m -> t two m", two=2)
                i = sy.dma_start(out=vdst[tt * 128:(tt + 1) * 128, half, :],
                                 in_=c.STG[:, tt % 2, :])
                tk = c.sIO.tok(i, 16)
                c.stg_trk[tt % 2].rdone(tk)
                vproj_toks.append(tk)

    agbox = []

    def kick_ag():
        @blk.gpsimd
        def _(g):
            waits(g, kproj_toks + vproj_toks)
            i = g.collective_compute(
                "AllGather", ALU.bypass,
                replica_groups=[[0, 1], [2, 3], [4, 5], [6, 7]],
                ins=[kvcon.opt()], outs=[kvg.opt()])
            agbox.append(agsem.tok(i))

    if AG_OVERLAP:
        kick_ag()
    if L == 0:
        emit_ff_l0(c)

    if DEBUG_ATTN and L == 0:
        @blk.sync
        def _(sy):
            waits(sy, [c.r_tok])
            i = sy.dma_start(out=c.dbg_rff.rearrange("(k p) m -> p k m", p=128), in_=c.R[:])
            c.misc["dbgr"] = c.sIO.tok(i, 16)

    for m in range(DC):
        slotbox = []

        @blk.gpsimd
        def _(g, m=m, slotbox=slotbox):
            if L == 0:
                src = din["wqkv"][:, m * 128:(m + 1) * 128]
            else:
                src = din["wq"][:, m * 128:(m + 1) * 128]
            slotbox.append(ws_issue(c, g, src.rearrange("(k p) m -> p k m", p=128),
                                    16 * 128))

        ps, trk = (PA, tA) if m % 2 == 0 else (PB, tB)

        @blk.tensor
        def _(t, m=m, ps=ps, trk=trk, slotbox=slotbox):
            slot = slotbox[0]
            trk.wr(t)
            c.ws_trk[slot].rd(t)
            waits(t, [c.xn_trk[k].w for k in range(DC)])
            lhsv = ws_view(c, slot, 16, 128)
            last = None
            for k in range(DC):
                for h0 in range(2):
                    sl = slice(h0 * 512, (h0 + 1) * 512)
                    last = t.matmul(ps[:, sl], lhsv[:, k, :], c.XN[:, k, sl],
                                    start=(k == 0), stop=(k == DC - 1))
            tk = c.sPE.tok(last)
            trk.wdone(tk)
            c.ws_trk[slot].rdone(tk)
            if m == DC - 1:
                for d in range(DC):
                    c.xn_trk[d].rdone(tk)

        @blk.scalar
        def _(s, m=m, ps=ps, trk=trk):
            trk.rd(s)
            c.biga_trk[m].wr(s)
            if L == 1:
                i = s.activation(c.BIGA[:, m, :], ps[:], AF.Identity,
                                 bias=c.BIAS[:, DOFF["sbq"] + m:DOFF["sbq"] + m + 1],
                                 scale=QSCALE)
            else:
                i = s.activation(c.BIGA[:, m, :], ps[:], AF.Identity,
                                 bias=0.0, scale=QSCALE)
            tk = c.sACT.tok(s.drain())
            trk.rdone(tk)
            c.biga_trk[m].wdone(tk)


    if not AG_OVERLAP:
        kick_ag()

    if DEBUG_ATTN and L == DBG_L:
        @blk.gpsimd
        def _(g):
            waits(g, kproj_toks + vproj_toks)
            i = g.dma_start(out=c.dbg_kvc, in_=kvcon)
            c.misc["dbg0"] = c.sWL.tok(i, 16)
            waits(g, agbox)
            i = g.dma_start(out=c.dbg_kvg, in_=kvg)
            c.misc["dbg1"] = c.sWL.tok(i, 16)
            waits(g, [c.biga_trk[m].w for m in range(16)])
            i = g.dma_start(out=c.dbg_q.rearrange("(k p) m -> p k m", p=128), in_=c.BIGA[:])
            tk = c.sWL.tok(i, 16)
            c.misc["dbg2"] = tk
            for m in range(16):
                c.biga_trk[m].rdone(tk)

    arg_trk = [Trk(), Trk()]
    for si_ in range(2):
        arg_trk[si_].w = c.dl_trk[si_].w
        arg_trk[si_].r = list(c.dl_trk[si_].r)
    pah_trk = [Trk(), Trk()]
    pah_trk[0].w = tA.w
    pah_trk[0].r = list(tA.r)
    pah_trk[1].w = tD.w
    pah_trk[1].r = list(tD.r)
    pslot_ctr = [0]
    ROWLANE = slice(2048, 3072)

    for h in range(H):
        @blk.sync
        def _(sy, h=h):
            c.kh_trk[h % 2].wr(sy)
            waits(sy, agbox)
            dst = c.KH[:, h % 2, :].rearrange("p (l two f) -> p l two f", two=2, f=128)
            i = sy.dma_start(out=dst[:, :, 1, :],
                             in_=kvg[h * 128:(h + 1) * 128, :]
                             .rearrange("p (l f) -> p l f", f=128))
            c.sIO.tok(i, 16)
            i = sy.dma_start(out=dst[:, :, 0, :],
                             in_=kvg[4096 + h * 128:4096 + (h + 1) * 128, :]
                             .rearrange("p (l f) -> p l f", f=128))
            c.kh_trk[h % 2].wdone(c.sIO.tok(i, 16))
            c.vth_trk[h % 2].wr(sy)
            vdst = c.VTH[:, h % 2].rearrange("p (l two) f -> p l two f", two=2)
            vsrc0 = kvg[2048:4096, :].rearrange(
                "(l p two) m -> p l (two m)", p=128, two=2)[:, :, (h // 8) * 1024 + (h % 8) * 128:
                                                           (h // 8) * 1024 + (h % 8) * 128 + 128]
            i = sy.dma_start(out=vdst[:, :, 1, :], in_=vsrc0)
            c.sIO.tok(i, 16)
            vsrc1 = kvg[4096 + 2048:8192, :].rearrange(
                "(l p two) m -> p l (two m)", p=128, two=2)[:, :, (h // 8) * 1024 + (h % 8) * 128:
                                                           (h // 8) * 1024 + (h % 8) * 128 + 128]
            i = sy.dma_start(out=vdst[:, :, 0, :], in_=vsrc1)
            c.vth_trk[h % 2].wdone(c.sIO.tok(i, 16))

        if L == 1:
            @blk.sync
            def _(sy, h=h):
                waits(sy, [c.misc.get("lane_free")])
                i = sy.dma_start(out=c.ROWS[0:1, ROWLANE],
                                 in_=din["crow"][0:1, h * 1024:(h + 1) * 1024])
                c.misc["lane_tok"] = c.sIO.tok(i, 16)

            @blk.tensor
            def _(t, h=h):
                tB.wr(t)
                waits(t, [c.misc["lane_tok"], c.misc["ones"]])
                for h0 in range(2):
                    sl = slice(h0 * 512, (h0 + 1) * 512)
                    i = t.matmul(PB[:, sl], c.ONESF[:],
                                 c.ROWS[0:1, 2048 + h0 * 512:2048 + (h0 + 1) * 512],
                                 start=True, stop=True)
                tk = c.sPE.tok(i)
                tB.wdone(tk)
                c.misc["lane_free"] = tk

            @blk.scalar
            def _(s, h=h):
                tB.rd(s)
                c.crb_trk.wr(s)
                i = s.activation(c.CRB[:], PB[:], AF.Identity, bias=0.0, scale=1.0)
                tk = c.sACT.tok(s.drain())
                tB.rdone(tk)
                c.crb_trk.wdone(tk)

            if DEBUG_ATTN and L == DBG_L and h == 0:
                @blk.sync
                def _(sy):
                    c.crb_trk.rd(sy)
                    i = sy.dma_start(out=c.dbg_h0[0:128, :], in_=c.CRB[:])
                    tk = c.sIO.tok(i, 16)
                    c.crb_trk.rdone(tk)
                    c.misc["dbgh"] = tk

        for kc in range(16):
            q0 = (kc // 2) * 128
            n = 1024 - q0
            slot = kc % 2
            ps = PA if slot == 0 else PD

            @blk.tensor
            def _(t, h=h, kc=kc, q0=q0, slot=slot, ps=ps):
                pah_trk[slot].wr(t)
                c.kh_trk[h % 2].rd(t)
                c.biga_trk[h].rd(t)
                lh = c.KH[:, h % 2, kc * 128:(kc + 1) * 128]
                if q0 < 512:
                    t.matmul(ps[:, q0:512], lh, c.BIGA[:, h, q0:512],
                             start=True, stop=True)
                    i = t.matmul(ps[:, 512:1024], lh, c.BIGA[:, h, 512:1024],
                                 start=True, stop=True)
                else:
                    i = t.matmul(ps[:, q0:1024], lh, c.BIGA[:, h, q0:1024],
                                 start=True, stop=True)
                tk = c.sPE.tok(i)
                pah_trk[slot].wdone(tk)
                if kc == 15:
                    c.kh_trk[h % 2].rdone(tk)

            if L == 1:
                @blk.vector
                def _(v, h=h, kc=kc, q0=q0, n=n, slot=slot, ps=ps):
                    pah_trk[slot].rd(v)
                    arg_trk[slot].wr(v)
                    c.crb_trk.rd(v)
                    aco = c.ACOL[:, h * 16 + kc:h * 16 + kc + 1]
                    argv = c.DELTA[:, slot, :]
                    i = v.scalar_tensor_tensor(
                        argv[:, q0:1024], ps[:, q0:1024], aco, c.CRB[:, q0:1024],
                        ALU.add, ALU.subtract)
                    msl = slice((kc % 2) * 128, (kc % 2) * 128 + 128)
                    i = v.tensor_tensor(argv[:, q0:q0 + 128], argv[:, q0:q0 + 128],
                                        c.AMASK[:, msl], ALU.add)
                    tk = c.sDVE.tok(i)
                    pah_trk[slot].rdone(tk)
                    arg_trk[slot].wdone(tk)
                    c.crb_trk.rdone(tk)

            @blk.scalar
            def _(s, h=h, kc=kc, q0=q0, slot=slot, ps=ps):
                c.pt_trk = getattr(c, "pt_trk", [Trk(), Trk()])
                c.pt_trk[slot].wr(s)
                if L == 1:
                    arg_trk[slot].rd(s)
                    i = s.activation(c.PT[:, slot, q0:1024],
                                     c.DELTA[:, slot, q0:1024],
                                     AF.Exp, bias=0.0, scale=1.0)
                    tk = c.sACT.tok(i)
                    arg_trk[slot].rdone(tk)
                else:
                    pah_trk[slot].rd(s)
                    i = s.activation(c.PT[:, slot, q0:1024], ps[:, q0:1024],
                                     AF.Exp, bias=0.0, scale=1.0)
                    tk = c.sACT.tok(i)
                    pah_trk[slot].rdone(tk)
                c.pt_trk[slot].wdone(tk)

            if L == 0:
                @blk.vector
                def _(v, kc=kc, q0=q0, slot=slot):
                    c.pt_trk[slot].rd(v)
                    msl = slice((kc % 2) * 128, (kc % 2) * 128 + 128)
                    i = v.tensor_tensor(c.PT[:, slot, q0:q0 + 128],
                                        c.PT[:, slot, q0:q0 + 128],
                                        c.MASK[:, msl], ALU.mult)
                    c.pt_trk[slot].wdone(c.sDVE.tok(i))

            @blk.tensor
            def _(t, h=h, kc=kc, q0=q0, slot=slot):
                c.pt_trk[slot].rd(t)
                c.vth_trk[h % 2].rd(t)
                if kc == 0:
                    tB.wr(t)
                    tC.wr(t)
                pieces = [(q0, 512), (512, 1024)] if q0 < 512 else [(q0, 1024)]
                i = None
                for (a, b) in pieces:
                    t.matmul(PB[:, a:b], c.VTH[:, h % 2, kc, :],
                             c.PT[:, slot, a:b],
                             start=(kc == 0), stop=False, skip_group_check=True)
                    i = t.matmul(PC[0:1, a:b], c.ONESC[:], c.PT[:, slot, a:b],
                                 start=(kc == 0), stop=False, skip_group_check=True)
                tk = c.sPE.tok(i)
                c.pt_trk[slot].rdone(tk)
                c.misc["last_avsum"] = tk

        @blk.tensor
        def _(t, h=h):
            pass

        @blk.vector
        def _(v, h=h):
            waits(v, [c.misc["last_avsum"]])
            i = v.reciprocal(c.ROWS[0:1, ROWLANE], PC[0:1, :])
            tk = c.sDVE.tok(i)
            tC.rdone(tk)
            c.misc["recip_tok"] = tk

        @blk.tensor
        def _(t, h=h):
            pah_trk[1].wr(t)
            waits(t, [c.misc["recip_tok"]])
            for h0 in range(2):
                sl = slice(h0 * 512, (h0 + 1) * 512)
                i = t.matmul(PD[:, sl], c.ONESF[:],
                             c.ROWS[0:1, 2048 + h0 * 512:2048 + (h0 + 1) * 512],
                             start=True, stop=True)
            tk = c.sPE.tok(i)
            pah_trk[1].wdone(tk)
            c.misc["lane_free"] = tk

        @blk.scalar
        def _(s, h=h):
            pah_trk[1].rd(s)
            arg_trk[0].wr(s)
            i = s.activation(c.DELTA[:, 0, :], PD[:], AF.Identity, bias=0.0, scale=1.0)
            tk = c.sACT.tok(s.drain())
            pah_trk[1].rdone(tk)
            arg_trk[0].wdone(tk)

        @blk.vector
        def _(v, h=h):
            arg_trk[0].rd(v)
            c.biga_trk[h].wr(v)
            i = v.tensor_tensor(c.BIGA[:, h, :], PB[:], c.DELTA[:, 0, :], ALU.mult)
            tk = c.sDVE.tok(i)
            tB.rdone(tk)
            arg_trk[0].rdone(tk)
            c.biga_trk[h].wdone(tk)
            c.vth_trk[h % 2].rdone(tk)

    if DEBUG_ATTN and L == DBG_L:
        @blk.gpsimd
        def _(g):
            waits(g, [c.biga_trk[m].w for m in range(16)])
            i = g.dma_start(out=c.dbg_ctx.rearrange("(k p) m -> p k m", p=128), in_=c.BIGA[:])
            tk = c.sWL.tok(i, 16)
            c.misc["dbg3"] = tk
            for m in range(16):
                c.biga_trk[m].rdone(tk)

    tA.r = tA.r + pah_trk[0].r + [t for t in (pah_trk[0].w,) if t]
    tD.r = tD.r + pah_trk[1].r + [t for t in (pah_trk[1].w,) if t]
    for si_ in range(2):
        c.dl_trk[si_].r = c.dl_trk[si_].r + arg_trk[si_].r + \
            [t for t in (arg_trk[si_].w,) if t]

    for m in range(DC):
        slotbox = []

        @blk.gpsimd
        def _(g, m=m, slotbox=slotbox):
            w = din["wproj"] if L == 0 else din["wo"]
            src = w[:, m * 128:(m + 1) * 128]
            slotbox.append(ws_issue(c, g, src.rearrange("(k p) m -> p k m", p=128),
                                    16 * 128))

        ps, trk = (PA, tA) if m % 2 == 0 else (PB, tB)

        @blk.tensor
        def _(t, m=m, ps=ps, trk=trk, slotbox=slotbox):
            slot = slotbox[0]
            trk.wr(t)
            c.ws_trk[slot].rd(t)
            for d in range(DC):
                c.biga_trk[d].rd(t)
            lhsv = ws_view(c, slot, 16, 128)
            last = None
            for k in range(DC):
                for h0 in range(2):
                    sl = slice(h0 * 512, (h0 + 1) * 512)
                    last = t.matmul(ps[:, sl], lhsv[:, k, :], c.BIGA[:, k, sl],
                                    start=(k == 0), stop=(k == DC - 1))
            tk = c.sPE.tok(last)
            trk.wdone(tk)
            c.ws_trk[slot].rdone(tk)
            if m == DC - 1:
                for d in range(DC):
                    c.biga_trk[d].rdone(tk)

        @blk.vector
        def _(v, m=m, ps=ps, trk=trk):
            trk.rd(v)
            if L == 1:
                bo = c.BIAS[:, BOFF["bo"] + m:BOFF["bo"] + m + 1]
                i = v.scalar_tensor_tensor(c.R[:, m, :], ps[:], bo, c.R[:, m, :],
                                           ALU.add, ALU.add)
            else:
                i = v.tensor_tensor(c.R[:, m, :], c.R[:, m, :], ps[:], ALU.add)
            tk = c.sDVE.tok(i)
            trk.rdone(tk)
            c.r_tok = tk


def emit_ff_l0(c):
    blk = c.block
    PA, PB, PC, PD = c.PS
    tA, tB, tC, tD = c.pstrk
    din = c.din

    for grp in range(8):
        gbase = grp % 2 * 8
        for hc8 in range(8):
            hc = grp * 8 + hc8
            slotbox = []

            @blk.gpsimd
            def _(g, hc=hc, slotbox=slotbox):
                src = din["wff1"][:, hc * 128:(hc + 1) * 128]
                slotbox.append(ws_issue(c, g, src.rearrange("(k p) m -> p k m", p=128),
                                        16 * 128))

            ps, trk = (PA, tA) if hc % 2 == 0 else (PB, tB)

            @blk.tensor
            def _(t, hc=hc, ps=ps, trk=trk, slotbox=slotbox):
                slot = slotbox[0]
                trk.wr(t)
                c.ws_trk[slot].rd(t)
                waits(t, [c.xn_trk[k].w for k in range(DC)])
                lhsv = ws_view(c, slot, 16, 128)
                last = None
                for k in range(DC):
                    for h0 in range(2):
                        sl = slice(h0 * 512, (h0 + 1) * 512)
                        last = t.matmul(ps[:, sl], lhsv[:, k, :], c.XN[:, k, sl],
                                        start=(k == 0), stop=(k == DC - 1))
                tk = c.sPE.tok(last)
                trk.wdone(tk)
                c.ws_trk[slot].rdone(tk)

            @blk.scalar
            def _(s, hc=hc, gbase=gbase, hc8=hc8, ps=ps, trk=trk):
                trk.rd(s)
                c.biga_trk[gbase + hc8].wr(s)
                i = s.activation(c.BIGA[:, gbase + hc8, :], ps[:], AF.Gelu,
                                 bias=c.BIAS[:, BOFF["bff1"] + hc:BOFF["bff1"] + hc + 1],
                                 scale=1.0)
                tk = c.sACT.tok(s.drain())
                trk.rdone(tk)
                c.biga_trk[gbase + hc8].wdone(tk)

        if DEBUG_ATTN and grp == 0:
            @blk.gpsimd
            def _(g):
                waits(g, [c.biga_trk[j].w for j in range(8)])
                i = g.dma_start(out=c.dbg_h0.rearrange("(k p) m -> p k m", p=128),
                                in_=c.BIGA[:, 0:8, :])
                tk = c.sWL.tok(i, 16)
                c.misc["dbgh"] = tk
                for j in range(8):
                    c.biga_trk[j].rdone(tk)

        for m in range(DC):
            slotbox = []

            @blk.gpsimd
            def _(g, grp=grp, m=m, slotbox=slotbox):
                src = din["wff2"][grp * 1024:(grp + 1) * 1024, m * 128:(m + 1) * 128]
                slotbox.append(ws_issue(c, g, src.rearrange("(k p) m -> p k m", p=128),
                                        8 * 128))

            ps, trk = (PC, tC) if m % 2 == 0 else (PD, tD)

            @blk.tensor
            def _(t, grp=grp, m=m, gbase=gbase, ps=ps, trk=trk, slotbox=slotbox):
                slot = slotbox[0]
                trk.wr(t)
                c.ws_trk[slot].rd(t)
                for j in range(8):
                    c.biga_trk[gbase + j].rd(t)
                lhsv = ws_view(c, slot, 8, 128)
                last = None
                for j in range(8):
                    for h0 in range(2):
                        sl = slice(h0 * 512, (h0 + 1) * 512)
                        last = t.matmul(ps[:, sl], lhsv[:, j, :],
                                        c.BIGA[:, gbase + j, sl],
                                        start=(j == 0), stop=(j == 7))
                tk = c.sPE.tok(last)
                trk.wdone(tk)
                c.ws_trk[slot].rdone(tk)
                if m == DC - 1:
                    for j in range(8):
                        c.biga_trk[gbase + j].rdone(tk)

            @blk.vector
            def _(v, grp=grp, m=m, ps=ps, trk=trk):
                trk.rd(v)
                if grp == 0:
                    b2 = c.BIAS[:, BOFF["bff2"] + m:BOFF["bff2"] + m + 1]
                    i = v.scalar_tensor_tensor(c.R[:, m, :], ps[:], b2, c.R[:, m, :],
                                               ALU.add, ALU.add)
                else:
                    i = v.tensor_tensor(c.R[:, m, :], c.R[:, m, :], ps[:], ALU.add)
                tk = c.sDVE.tok(i)
                trk.rdone(tk)
                c.r_tok = tk



def emit_moe(c):
    blk = c.block
    PA, PB, PC, PD = c.PS
    tA, tB, tC, tD = c.pstrk
    din = c.din
    if not hasattr(c, "biga_trk"):
        c.biga_trk = [Trk() for _ in range(16)]

    emit_ln(c, add_n1=False)

    @blk.gpsimd
    def _(g):
        if not hasattr(c, "wrt_trk"):
            c.wrt_trk = Trk()
        c.wrt_trk.wr(g)
        i = g.dma_start(out=c.WRTS.rearrange("p (k e) -> p k e", e=8),
                        in_=din["wrt"].rearrange("(k p) e -> p k e", p=128))
        c.wrt_trk.wdone(c.sWL.tok(i, 16))

    @blk.tensor
    def _(t):
        tC.wr(t)
        c.wrt_trk.rd(t)
        waits(t, [c.xn_trk[k].w for k in range(DC)])
        last = None
        for tt in range(NT):
            for k in range(DC):
                last = t.matmul(PC[:, tt * 8:(tt + 1) * 8],
                                c.XN[:, k, tt * 128:(tt + 1) * 128],
                                c.WRTS[:, k * 8:(k + 1) * 8],
                                start=(k == 0), stop=(k == DC - 1),
                                skip_group_check=True)
        tk = c.sPE.tok(last)
        tC.wdone(tk)
        c.wrt_trk.rdone(tk)

    @blk.scalar
    def _(s):
        tC.rd(s)
        i = s.activation(c.LG[:], PC[:, 0:64], AF.Identity, bias=0.0, scale=1.0)
        tk = c.sACT.tok(s.drain())
        tC.rdone(tk)
        c.misc["rt0"] = tk

    D64 = c.DELTA[:, 1, 0:64]
    lg3 = c.LG.rearrange("p (t e) -> p t e", e=8)

    @blk.vector
    def _(v):
        waits(v, [c.misc["rt0"]])
        c.dl_trk[1].wr(v)
        i = v.tensor_reduce(c.MRT[:, 0, :], lg3, AX.X, ALU.max)
        waits(v, [c.sDVE.tok(i)])
        i = None
        for g8 in range(8):
            i = v.tensor_scalar(c.W64[:, g8 * 8:(g8 + 1) * 8], c.LG[:, g8 * 8:(g8 + 1) * 8],
                            c.MRT[:, 0, g8:g8 + 1], None, ALU.subtract)
        c.misc["rt1"] = c.sDVE.tok(i)

    @blk.scalar
    def _(s):
        waits(s, [c.misc["rt1"]])
        i = s.activation(c.LG[:], c.W64[:], AF.Exp, bias=0.0, scale=1.0)
        c.misc["rt2"] = c.sACT.tok(s.drain())

    @blk.vector
    def _(v):
        waits(v, [c.misc["rt2"]])
        i = v.tensor_reduce(c.MRT[:, 1, :], lg3, AX.X, ALU.add)
        c.misc["rt2b"] = c.sDVE.tok(i)

    @blk.scalar
    def _(s):
        waits(s, [c.misc["rt2b"]])
        s.activation(c.MRT[:, 5, :], c.MRT[:, 1, :], AF.Ln, bias=0.0, scale=1.0)
        i = s.activation(c.MRT[:, 2, :], c.MRT[:, 5, :], AF.Exp, bias=0.0, scale=-1.0)
        c.misc["rt2c"] = c.sACT.tok(s.drain())

    @blk.vector
    def _(v):
        waits(v, [c.misc["rt2c"]])
        for g8 in range(8):
            v.tensor_scalar(c.W64[:, g8 * 8:(g8 + 1) * 8], c.LG[:, g8 * 8:(g8 + 1) * 8],
                            c.MRT[:, 2, g8:g8 + 1], None, ALU.mult)
        w3 = c.W64.rearrange("p (t e) -> p t e", e=8)
        i = v.tensor_reduce(c.MRT[:, 3, :], w3, AX.X, ALU.max)
        waits(v, [c.sDVE.tok(i)])
        i = None
        for g8 in range(8):
            i = v.tensor_scalar(c.LG[:, g8 * 8:(g8 + 1) * 8], c.W64[:, g8 * 8:(g8 + 1) * 8],
                            c.MRT[:, 3, g8:g8 + 1], None, ALU.subtract)
        c.misc["rt3"] = c.sDVE.tok(i)

    @blk.scalar
    def _(s):
        waits(s, [c.misc["rt3"]])
        s.sign(c.LG[:], c.LG[:], bias=0.0)
        i = s.activation(c.LG[:], c.LG[:], AF.Relu, bias=0.0, scale=-1.0)
        c.misc["rt4"] = c.sACT.tok(s.drain())

    @blk.vector
    def _(v):
        waits(v, [c.misc["rt4"]])
        v.tensor_tensor(D64, c.W64[:], c.LG[:], ALU.mult)
        d3 = c.DELTA[:, 1, 0:64].rearrange("p (t e) -> p t e", e=8)
        i = v.tensor_reduce(c.MRT[:, 4, :], d3, AX.X, ALU.max)
        waits(v, [c.sDVE.tok(i)])
        for g8 in range(8):
            v.tensor_scalar(c.LG[:, g8 * 8:(g8 + 1) * 8], c.W64[:, g8 * 8:(g8 + 1) * 8],
                            c.MRT[:, 4, g8:g8 + 1], None, ALU.is_ge)
        v.tensor_tensor(c.W64[:], c.W64[:], c.LG[:], ALU.mult)
        i = v.tensor_scalar(c.W64[:], c.W64[:], 0.5, None, ALU.mult)
        tk = c.sDVE.tok(i)
        c.misc["w64"] = tk
        c.dl_trk[1].wdone(tk)

    def shared_pass(fs_list, koff, kcnt):
        for idx, fs in enumerate(fs_list):
            slots = []

            @blk.gpsimd
            def _(g, fs=fs, slots=slots):
                src = din["wsgt"][:, fs * 128:(fs + 1) * 128]
                slots.append(ws_issue(c, g, src.rearrange("(k p) m -> p k m", p=128),
                                      16 * 128))
                src = din["wsut"][:, fs * 128:(fs + 1) * 128]
                slots.append(ws_issue(c, g, src.rearrange("(k p) m -> p k m", p=128),
                                      16 * 128))

            @blk.tensor
            def _(t, fs=fs, slots=slots):
                tA.wr(t)
                tB.wr(t)
                c.ws_trk[slots[0]].rd(t)
                c.ws_trk[slots[1]].rd(t)
                waits(t, [c.xn_trk[k].w for k in range(DC)])
                lg_ = ws_view(c, slots[0], 16, 128)
                lu_ = ws_view(c, slots[1], 16, 128)
                lastg = lastu = None
                for k in range(DC):
                    for h0 in range(2):
                        sl = slice(h0 * 512, (h0 + 1) * 512)
                        lastg = t.matmul(PA[:, sl], lg_[:, k, :], c.XN[:, k, sl],
                                         start=(k == 0), stop=(k == DC - 1))
                    for h0 in range(2):
                        sl = slice(h0 * 512, (h0 + 1) * 512)
                        lastu = t.matmul(PB[:, sl], lu_[:, k, :], c.XN[:, k, sl],
                                         start=(k == 0), stop=(k == DC - 1))
                tkg = c.sPE.tok(lastg)
                tku = c.sPE.tok(lastu)
                tA.wdone(tkg)
                tB.wdone(tku)
                c.ws_trk[slots[0]].rdone(tkg)
                c.ws_trk[slots[1]].rdone(tku)

            @blk.scalar
            def _(s, fs=fs):
                tA.rd(s)
                if not hasattr(c, "sgb_trk"):
                    c.sgb_trk = Trk()
                c.sgb_trk.wr(s)
                i = s.activation(c.SGB[:], PA[:], AF.Silu, bias=0.0, scale=1.0)
                tk = c.sACT.tok(s.drain())
                tA.rdone(tk)
                c.sgb_trk.wdone(tk)

            @blk.vector
            def _(v, fs=fs, idx=idx):
                tB.rd(v)
                c.sgb_trk.rd(v)
                dst = idx if len(fs_list) == 6 else fs
                c.biga_trk[dst].wr(v)
                i = v.tensor_tensor(c.BIGA[:, dst, :], c.SGB[:], PB[:], ALU.mult)
                tk = c.sDVE.tok(i)
                tB.rdone(tk)
                c.sgb_trk.rdone(tk)
                c.biga_trk[dst].wdone(tk)

        for m in range(DC):
            slots = []
            psd, trkd = (PC, tC) if m % 2 == 0 else (PD, tD)

            @blk.gpsimd
            def _(g, m=m, slots=slots):
                src = din["wsdt"][koff:koff + kcnt * 128, m * 128:(m + 1) * 128]
                slots.append(ws_issue(c, g, src.rearrange("(k p) m -> p k m", p=128),
                                      kcnt * 128))

            @blk.tensor
            def _(t, m=m, slots=slots, psd=psd, trkd=trkd):
                trkd.wr(t)
                c.ws_trk[slots[0]].rd(t)
                for j in range(kcnt):
                    c.biga_trk[j].rd(t)
                lw = ws_view(c, slots[0], kcnt, 128)
                last = None
                for j in range(kcnt):
                    for h0 in range(2):
                        sl = slice(h0 * 512, (h0 + 1) * 512)
                        last = t.matmul(psd[:, sl], lw[:, j, :], c.BIGA[:, j, sl],
                                        start=(j == 0), stop=(j == kcnt - 1))
                tk = c.sPE.tok(last)
                trkd.wdone(tk)
                c.ws_trk[slots[0]].rdone(tk)
                if m == DC - 1:
                    for j in range(kcnt):
                        c.biga_trk[j].rdone(tk)

            @blk.vector
            def _(v, m=m, psd=psd, trkd=trkd):
                trkd.rd(v)
                i = v.scalar_tensor_tensor(c.R[:, m, :], psd[:], 0.5, c.R[:, m, :],
                                           ALU.mult, ALU.add)
                tk = c.sDVE.tok(i)
                trkd.rdone(tk)
                c.r_tok = tk

    if DEBUG_ATTN:
        @blk.sync
        def _(sy):
            waits(sy, [c.misc["w64"]])
            i = sy.dma_start(out=c.dbg_w64, in_=c.W64[:])
            c.misc["dbgw"] = c.sIO.tok(i, 16)

    shared_pass(list(range(16)), 0, 16)
    shared_pass(list(range(16, 22)), 16 * 128, 6)

    if DEBUG_ATTN:
        @blk.sync
        def _(sy):
            waits(sy, [c.r_tok])
            i = sy.dma_start(out=c.dbg_rsh.rearrange("(k p) m -> p k m", p=128), in_=c.R[:])
            c.misc["dbgrs"] = c.sIO.tok(i, 16)

    for e2 in range(E):
        @blk.vector
        def _(v, e2=e2):
            waits(v, [c.misc["w64"]])
            c.dl_trk[1].wr(v)
            i = None
            for tt in range(NT):
                i = v.tensor_scalar(c.DELTA[:, 1, tt * 128:(tt + 1) * 128],
                                    c.DELTA[:, 1, tt * 128:(tt + 1) * 128],
                                    0.0, c.W64[:, tt * 8 + e2:tt * 8 + e2 + 1],
                                    ALU.mult, op1=ALU.add)
            tk = c.sDVE.tok(i)
            c.misc["wb_src"] = tk
            c.dl_trk[1].wdone(tk)

        @blk.tensor
        def _(t, e2=e2):
            tC.wr(t)
            waits(t, [c.misc["wb_src"], c.misc["idf"]])
            last = None
            for tt in range(NT):
                last = t.transpose(PC[:, tt * 128:(tt + 1) * 128],
                                   c.DELTA[:, 1, tt * 128:(tt + 1) * 128], c.IDF[:])
            tk = c.sPE.tok(last)
            tC.wdone(tk)
            c.dl_trk[1].rdone(tk)

        for fc in range(FEC):
            slots = []

            @blk.gpsimd
            def _(g, e2=e2, fc=fc, slots=slots):
                src = din["wgt"][e2][:, fc * 128:(fc + 1) * 128]
                slots.append(ws_issue(c, g, src.rearrange("(k p) m -> p k m", p=128),
                                      16 * 128))
                src = din["wut"][e2][:, fc * 128:(fc + 1) * 128]
                slots.append(ws_issue(c, g, src.rearrange("(k p) m -> p k m", p=128),
                                      16 * 128))

            @blk.tensor
            def _(t, fc=fc, slots=slots):
                tA.wr(t)
                tB.wr(t)
                c.ws_trk[slots[0]].rd(t)
                c.ws_trk[slots[1]].rd(t)
                waits(t, [c.xn_trk[k].w for k in range(DC)])
                lg_ = ws_view(c, slots[0], 16, 128)
                lu_ = ws_view(c, slots[1], 16, 128)
                lastg = lastu = None
                for k in range(DC):
                    for h0 in range(2):
                        sl = slice(h0 * 512, (h0 + 1) * 512)
                        lastg = t.matmul(PA[:, sl], lg_[:, k, :], c.XN[:, k, sl],
                                         start=(k == 0), stop=(k == DC - 1))
                    for h0 in range(2):
                        sl = slice(h0 * 512, (h0 + 1) * 512)
                        lastu = t.matmul(PB[:, sl], lu_[:, k, :], c.XN[:, k, sl],
                                         start=(k == 0), stop=(k == DC - 1))
                tkg = c.sPE.tok(lastg)
                tku = c.sPE.tok(lastu)
                tA.wdone(tkg)
                tB.wdone(tku)
                c.ws_trk[slots[0]].rdone(tkg)
                c.ws_trk[slots[1]].rdone(tku)
                if fc == FEC - 1:
                    pass

            @blk.scalar
            def _(s, fc=fc):
                tA.rd(s)
                c.sgb_trk.wr(s)
                i = s.activation(c.SGB[:], PA[:], AF.Silu, bias=0.0, scale=1.0)
                tk = c.sACT.tok(s.drain())
                tA.rdone(tk)
                c.sgb_trk.wdone(tk)

            @blk.vector
            def _(v, fc=fc):
                tB.rd(v)
                c.sgb_trk.rd(v)
                if not hasattr(c, "hb_trk"):
                    c.hb_trk = Trk()
                c.hb_trk.wr(v)
                v.tensor_tensor(c.HB[:], c.SGB[:], PB[:], ALU.mult)
                tkp = (c.sDVE, c.sDVE.n + 1)
                tC.rd(v)
                c.biga_trk[fc].wr(v)
                i = v.tensor_tensor(c.BIGA[:, fc, :], c.HB[:], PC[:], ALU.mult)
                tk = c.sDVE.tok(i)
                tB.rdone(tk)
                c.sgb_trk.rdone(tk)
                c.hb_trk.wdone(tk)
                tC.rdone(tk)
                c.biga_trk[fc].wdone(tk)

        for m in range(DC):
            slots = []
            psd, trkd = (PD, tD) if m % 2 == 0 else (PB, tB)

            @blk.gpsimd
            def _(g, e2=e2, m=m, slots=slots):
                src = din["wdt"][e2][:, m * 128:(m + 1) * 128]
                slots.append(ws_issue(c, g, src.rearrange("(k p) m -> p k m", p=128),
                                      FEC * 128))

            @blk.tensor
            def _(t, m=m, slots=slots, psd=psd, trkd=trkd):
                trkd.wr(t)
                c.ws_trk[slots[0]].rd(t)
                for j in range(FEC):
                    c.biga_trk[j].rd(t)
                lw = ws_view(c, slots[0], FEC, 128)
                last = None
                for j in range(FEC):
                    for h0 in range(2):
                        sl = slice(h0 * 512, (h0 + 1) * 512)
                        last = t.matmul(psd[:, sl], lw[:, j, :], c.BIGA[:, j, sl],
                                        start=(j == 0), stop=(j == FEC - 1))
                tk = c.sPE.tok(last)
                trkd.wdone(tk)
                c.ws_trk[slots[0]].rdone(tk)
                if m == DC - 1:
                    for j in range(FEC):
                        c.biga_trk[j].rdone(tk)

            @blk.vector
            def _(v, m=m, e2=e2, psd=psd, trkd=trkd):
                trkd.rd(v)
                i = v.tensor_tensor(c.R[:, m, :], c.R[:, m, :], psd[:], ALU.add)
                tk = c.sDVE.tok(i)
                trkd.rdone(tk)
                c.r_tok = tk
                if m == DC - 1 and e2 == E - 1:
                    for k in range(DC):
                        c.xn_trk[k].rdone(tk)



_PHASES = ["pre0", "attn0", "post0", "moe", "attn1", "moe"]


def kernel(**inputs):
    maps = host_prep(inputs)
    nc, needed = build_program(_PHASES, first=True)
    in_maps = [{k: m[k] for k in needed} for m in maps]
    res = run_bass_kernel_spmd(nc, in_maps, core_ids=list(range(8)))
    return host_assemble(res.results)
